# version 04
# baseline (speedup 1.0000x reference)
import math
import numpy as np
import ml_dtypes
import concourse.bass as bass
import concourse.mybir as mybir
from concourse.bass_utils import run_bass_kernel_spmd

F32 = mybir.dt.float32
BF16 = mybir.dt.bfloat16
AF = mybir.ActivationFunctionType
ALU = mybir.AluOpType

NCORE = 8
D = 2048
DC = 16
T = 2048
CH = 1024
HALO = 512
INTER = 2048
EPS = 1e-5
SCALE = 1.0 / math.sqrt(128.0)
ES = [3584, 4096, 7680, 8192]
SWAP_MASK = [i ^ 1 for i in range(32)]

DEBUG_TAPS = ()


def _rope_tabs(positions, scale=1.0):
    pos = np.asarray(positions, dtype=np.float64)
    k = np.arange(64, dtype=np.float64)
    invf = 10000.0 ** (-2.0 * k / 128.0)
    th = pos[None, :] * invf[:, None]
    c, s = np.cos(th), np.sin(th)
    A = np.empty((128, len(pos)), np.float64)
    Bt = np.empty((128, len(pos)), np.float64)
    A[0::2] = c; A[1::2] = c
    Bt[0::2] = -s; Bt[1::2] = s
    return (A * scale).astype(np.float32), (Bt * scale).astype(np.float32)


def _l0_masks(pA, pB):
    out = np.zeros((16, 128, 1024), np.float32)
    for ci, base in enumerate((pA, pB)):
        for qg in range(2):
            q0 = base + 512 * qg
            qabs = q0 + np.arange(512)
            for pr in range(4):
                for half in range(2):
                    kt = 2 * pr + half
                    kabs = (q0 - 512 + kt * 128) + np.arange(128)
                    valid = ((kabs[:, None] <= qabs[None, :])
                             & (kabs[:, None] > qabs[None, :] - 512)
                             & (kabs[:, None] >= 0))
                    out[ci * 8 + qg * 4 + pr, :, half * 512:(half + 1) * 512] = valid
    return out.astype(ml_dtypes.bfloat16)


def _host_core_inputs(core, ins):
    b, c = core // 4, core % 4
    pA, pB = c * CH, (7 - c) * CH
    x = np.asarray(ins["L0_attn_x"])
    x0_fm = np.ascontiguousarray(
        np.concatenate([x[b, pA:pA + CH], x[b, pB:pB + CH]], 0).T)
    hA = np.zeros((HALO, D), np.float32) if pA == 0 else x[b, pA - HALO:pA]
    hB = x[b, pB - HALO:pB]
    xh_fm = np.ascontiguousarray(np.concatenate([hA, hB], 0).T)

    own_pos = np.concatenate([pA + np.arange(CH), pB + np.arange(CH)])
    kposA = np.concatenate([np.maximum(pA - HALO + np.arange(HALO), 0),
                            pA + np.arange(CH)])
    kposB = np.concatenate([pB - HALO + np.arange(HALO), pB + np.arange(CH)])
    qa0, qb0 = _rope_tabs(own_pos, SCALE)
    ka0A, kb0A = _rope_tabs(kposA)
    ka0B, kb0B = _rope_tabs(kposB)
    qa1, qb1 = _rope_tabs(own_pos, SCALE)
    ka1, kb1 = _rope_tabs(own_pos)

    qabs = np.stack([pA + np.arange(512), pA + 512 + np.arange(512),
                     pB + np.arange(512), pB + 512 + np.arange(512)]
                    ).astype(np.float32)

    cat = np.concatenate
    return {
        "x0_fm": x0_fm, "xh_fm": xh_fm,
        "qa0": qa0, "qb0": qb0,
        "ka0": cat([ka0A, ka0B], 1), "kb0": cat([kb0A, kb0B], 1),
        "qa1": qa1, "qb1": qb1, "ka1": ka1, "kb1": kb1,
        "maskl0": _l0_masks(pA, pB),
        "qabs": qabs,
        "ident": np.eye(128, dtype=np.float32),
        "egate": np.full((128, 1), 1.0 if core == 0 else 0.0, np.float32),
        "wq0T": np.ascontiguousarray(np.asarray(ins["L0_attn_Wq"]).T),
        "wk0T": np.ascontiguousarray(np.asarray(ins["L0_attn_Wk"]).T),
        "wv0T": np.ascontiguousarray(np.asarray(ins["L0_attn_Wv"]).T),
        "wo0T": np.ascontiguousarray(np.asarray(ins["L0_attn_Wo"]).T),
        "w1T": np.ascontiguousarray(np.asarray(ins["L0_ffn_w1"]).T),
        "w3T": np.ascontiguousarray(np.asarray(ins["L0_ffn_w3"]).T),
        "w2T": np.ascontiguousarray(np.asarray(ins["L0_ffn_w2"]).T),
        "wrT": np.ascontiguousarray(np.asarray(ins["L0_ffn_wr"]).T),
        "wq1": np.ascontiguousarray(np.asarray(ins["L1_attn_Wq"])),
        "bq1": np.asarray(ins["L1_attn_bq"]).reshape(1, -1).copy(),
        "wk1": np.ascontiguousarray(np.asarray(ins["L1_attn_Wk"])),
        "bk1": np.asarray(ins["L1_attn_bk"]).reshape(1, -1).copy(),
        "wv1": np.ascontiguousarray(np.asarray(ins["L1_attn_Wv"])),
        "bv1": np.asarray(ins["L1_attn_bv"]).reshape(1, -1).copy(),
        "wo1": np.ascontiguousarray(np.asarray(ins["L1_attn_Wo"])),
        "bo1": np.asarray(ins["L1_attn_bo"]).reshape(1, -1).copy(),
        "bocol": np.ascontiguousarray(
            np.asarray(ins["L1_attn_bo"]).reshape(16, 128).T),
        "bbcol": np.ascontiguousarray(
            np.asarray(ins["L1_ffn_b_base"]).reshape(16, 128).T),
        "wbT": np.ascontiguousarray(np.asarray(ins["L1_ffn_W_base"]).T),
        "bb1": np.asarray(ins["L1_ffn_b_base"]).reshape(1, -1).copy(),
        "rwT": np.ascontiguousarray(np.asarray(ins["L1_ffn_router_W"]).T),
        "aT": np.ascontiguousarray(
            np.asarray(ins["L1_ffn_A"]).transpose(2, 0, 1).reshape(D, 128)),
        "bwT": np.ascontiguousarray(
            np.asarray(ins["L1_ffn_Bw"]).transpose(0, 2, 1).reshape(128, D)),
    }


ENGS = ("sync", "scalar", "vector", "tensor", "gpsimd")


class Sched:
    def __init__(self, nc):
        self.nc = nc
        self.ops = {e: [] for e in ENGS}
        self._ctx = []
        self.sems = {}
        self.cnt = {}
        self.guard = {}
        self.rope_ev = None
        self.autochain = {}

    def sem(self, name):
        h = self.nc.semaphore(name)
        self.sems[name] = h.__enter__()
        self._ctx.append(h)
        self.cnt[name] = 0
        return name

    def op(self, eng, emit, waits=(), inc=None, inc_by=1):
        waits = tuple(w for w in waits if w is not None)
        auto = self.autochain.get(eng)
        if auto is not None and emit is not None:
            prev = self.cnt[auto]
            if prev > 0:
                waits = waits + ((auto, prev),)
            if inc is None:
                inc = auto
        ev = None
        if inc is not None:
            self.cnt[inc] += inc_by
            ev = (inc, self.cnt[inc])
        self.ops[eng].append((waits, emit, (inc, inc_by) if inc else None))
        return ev

    def dma(self, eng, out, in_, waits=(), inc=None, **kw):
        return self.op(eng, lambda e: e.dma_start(out=out, in_=in_, **kw),
                       waits, inc, 16)

    def wait(self, eng, *evs):
        evs = tuple(e for e in evs if e is not None)
        if evs:
            self.ops[eng].append((evs, None, None))

    def now(self, sem):
        return (sem, self.cnt[sem])

    def finalize(self, block):
        for eng_name in ENGS:
            my_ops = self.ops[eng_name]

            def run(eng, _ops=my_ops):
                for waits, emit, inc in _ops:
                    for (s, v) in waits:
                        eng.wait_ge(self.sems[s], v)
                    if emit is None:
                        assert inc is None
                        continue
                    inst = emit(eng)
                    if inc is not None:
                        inst.then_inc(self.sems[inc[0]], inc[1])
            getattr(block, eng_name)(run)

    def close(self):
        for h in reversed(self._ctx):
            h.__exit__(None, None, None)


def build_program(debug_taps=(), limit=99):
    nc = bass.Bass("TRN2", target_bir_lowering=False)

    def din(name, shape, dt=F32):
        return nc.dram_tensor(name, shape, dt, kind="ExternalInput").ap()

    x0_fm = din("x0_fm", [D, T]);      xh_fm = din("xh_fm", [D, 1024])
    qa0 = din("qa0", [128, T]);        qb0 = din("qb0", [128, T])
    ka0 = din("ka0", [128, 3072]);     kb0 = din("kb0", [128, 3072])
    qa1 = din("qa1", [128, T]);        qb1 = din("qb1", [128, T])
    ka1 = din("ka1", [128, T]);        kb1 = din("kb1", [128, T])
    maskl0 = din("maskl0", [16, 128, 1024], BF16)
    qabs_i = din("qabs", [4, 512])
    ident_i = din("ident", [128, 128])
    egate_i = din("egate", [128, 1])
    wq0T = din("wq0T", [D, D]); wk0T = din("wk0T", [D, D])
    wv0T = din("wv0T", [D, D]); wo0T = din("wo0T", [D, D])
    w1T = din("w1T", [D, INTER]); w3T = din("w3T", [INTER, INTER])
    w2T = din("w2T", [INTER, D]); wrT = din("wrT", [D, 8])
    wq1 = din("wq1", [D, D]); bq1 = din("bq1", [1, D])
    wk1 = din("wk1", [D, 512]); bk1 = din("bk1", [1, 512])
    wv1 = din("wv1", [D, 512]); bv1 = din("bv1", [1, 512])
    wo1 = din("wo1", [512, D]); bo1 = din("bo1", [1, D])
    wbT = din("wbT", [D, D]); bb1 = din("bb1", [1, D])
    bocol = din("bocol", [128, 16]); bbcol = din("bbcol", [128, 16])
    rwT = din("rwT", [D, 8])
    aT = din("aT", [D, 128])
    bwT_i = din("bwT", [128, D])

    xout = nc.dram_tensor("xout", [D, T], F32, kind="ExternalOutput").ap()

    xnh_scr = nc.dram_tensor("xnh_scr", [D, 1024], BF16).ap()
    k0_scr = nc.dram_tensor("k0_scr", [D, 3072], BF16).ap()
    v0_scr = nc.dram_tensor("v0_scr", [3072, D], BF16).ap()
    y0_scr = nc.dram_tensor("y0_scr", [D, T], BF16).ap()
    x1_scr = nc.dram_tensor("x1_scr", [D, T], F32).ap()
    wqp_scr = nc.dram_tensor("wqp_scr", [D, 512], BF16).ap()
    kvag_src = nc.dram_tensor("kvag_src", [1024, D], BF16).ap()
    kvag_dst = nc.dram_tensor("kvag_dst", [4096, D], BF16).ap()
    eag_src = nc.dram_tensor("eag_src", [8, D], F32).ap()
    eag_dst = nc.dram_tensor("eag_dst", [8, D], F32).ap()

    ctx = []

    def sbuf(name, shape, dt):
        h = nc.sbuf_tensor(name, shape, dt)
        t = h.__enter__(); ctx.append(h)
        return t

    BIGA = sbuf("biga", [128, DC * T], BF16)
    BUF1 = sbuf("buf1", [128, DC * 1024], BF16)
    BUF2 = sbuf("buf2", [128, DC * 1024], BF16)
    WST = [sbuf(f"wst{i}", [128, T], F32) for i in range(2)]
    WBF = [sbuf(f"wbf{i}", [128, T], BF16) for i in range(2)]
    PMB = [sbuf(f"pmb{i}", [128, 1024], BF16) for i in range(2)]
    KWB = [sbuf(f"kwb{i}", [128, 1024], BF16) for i in range(2)]
    VWB = [sbuf(f"vwb{i}", [128, 1024], BF16) for i in range(2)]
    TAB = sbuf("tab", [128, 5120], BF16)
    RS = sbuf("rs", [65, 1536], F32)
    BIASR = sbuf("biasr", [65, 2048], BF16)
    ONES1 = sbuf("ones1", [65, 512], BF16)
    ONESC = sbuf("onesc", [128, 1], BF16)
    ONES1F = sbuf("ones1f", [65, 128], F32)
    IDB = sbuf("idb", [128, 128], BF16)
    IDF8 = sbuf("idf8", [8, 8], F32)
    EGT = sbuf("egt", [128, 1], F32)
    SML = sbuf("sml", [128, 64], F32)
    SML2 = sbuf("sml2", [128, 64], F32)
    GT = sbuf("gt", [128, 16 * 8], BF16)
    OHB = sbuf("ohb", [128, 128], BF16)
    RBQ = [sbuf(f"rbq{i}", [128, 512], F32) for i in range(2)]
    EVB = [sbuf(f"evb{i}", [128, 512], BF16) for i in range(3)]
    RP = [sbuf(f"rp{i}", [128, 512], BF16) for i in range(3)]
    ESUM = sbuf("esum", [128, 128], F32)
    FSC = sbuf("fsc", [1, 8], F32)

    sb_scoped = {}

    def sbuf_scoped(name, shape, dt):
        h = nc.sbuf_tensor(name, shape, dt)
        t = h.__enter__(); sb_scoped[name] = h
        return t

    def sbuf_free(*names):
        for n in names:
            sb_scoped.pop(n).__exit__(None, None, None)

    sc = Sched(nc)
    PE = sc.sem("pe"); DV = sc.sem("dv"); AC = sc.sem("ac"); GP = sc.sem("gp")
    CC1 = sc.sem("cc1"); CC2 = sc.sem("cc2")
    sc.autochain = {"vector": DV, "scalar": AC}
    QW = [sc.sem("qw0"), sc.sem("qw1")]
    QF = [sc.sem("qf0"), sc.sem("qf1")]
    QK = [sc.sem("qk0"), sc.sem("qk1")]
    QV = [sc.sem("qv0"), sc.sem("qv1")]
    QM = sc.sem("qm")
    QH = sc.sem("qh")
    QE = sc.sem("qe")
    QR = sc.sem("qr")
    QX = [sc.sem("qx0"), sc.sem("qx1")]
    QMS = sc.sem("qms")
    QSR = sc.sem("qsr")
    QSY = sc.sem("qsy")
    QSX = [sc.sem("qsx0"), sc.sem("qsx1")]
    QSO = [sc.sem("qso0"), sc.sem("qso1")]
    QSH = sc.sem("qsh")
    QSE = sc.sem("qse")
    ALL_G_STORES = (QSR, QSY, QSX[0], QSX[1], QSO[0], QSO[1], QSH, QSE)

    def g_barrier():
        for s in ALL_G_STORES:
            sc.wait("gpsimd", sc.now(s))
        return sc.op("gpsimd", lambda e: e.memset(FSC[0:1, 3:4], 0.0), inc=GP)
    FSEM = {e: sc.sem("f_" + e) for e in ("scalar", "vector", "tensor", "gpsimd")}

    def fence():
        evs = []
        evs.append(sc.op("vector", lambda e: e.memset(FSC[0:1, 0:1], 0.0),
                         inc=FSEM["vector"]))
        evs.append(sc.op("scalar", lambda e: e.activation(
            out=FSC[0:1, 1:2], in_=FSC[0:1, 1:2], func=AF.Copy),
            inc=FSEM["scalar"]))
        evs.append(sc.op("tensor", lambda e: e.nop(), inc=FSEM["tensor"]))
        evs.append(sc.op("gpsimd", lambda e: e.memset(FSC[0:1, 2:3], 0.0),
                         inc=FSEM["gpsimd"]))
        names = ("f_vector", "f_scalar", "f_tensor", "f_gpsimd")
        for e in ("vector", "scalar", "tensor", "gpsimd", "sync"):
            sc.wait(e, *[ev for ev in evs if ev[0] != "f_" + e])

    psum_live = []

    def psum(name, shape, dt=F32):
        h = nc.psum_tensor(name, shape, dt)
        t = h.__enter__(); psum_live.append(h)
        return t

    def psum_free():
        for h in reversed(psum_live):
            h.__exit__(None, None, None)
        psum_live.clear()

    def dram_tap(name, src_ap, waits):
        if name in debug_taps:
            shp = list(src_ap.shape)
            t = nc.dram_tensor("tap_" + name, shp, src_ap.dtype,
                               kind="ExternalOutput").ap()
            sc.dma("gpsimd", t, src_ap, waits=waits, inc=QSE)

    def load_wst(slot, src_ap, waits=(), ncols=None):
        g = sc.guard.get(("wst", slot))
        ev = sc.dma("sync", WST[slot][:, 0:ncols] if ncols else WST[slot][:],
                    src_ap, waits=tuple(waits) + (g,), inc=QW[slot])
        return ev

    def wst_used(slot, ev):
        sc.guard[("wst", slot)] = ev

    def colblock_ap(wT, O, o, nin=DC):
        return bass.AP(wT.tensor, o * 128, [[O, 128], [128 * O, nin], [1, 128]])

    def fm_proj(wT, O, rhs_fn, t_widths, consume, tag, wait_rhs=(),
                bias_slice=None, extra_fn=None, n_in=DC, src_bf16=False):
        nt = len(t_widths)
        assert nt <= 4
        mx = max(t_widths)
        hnd = [nc.psum_tensor(f"pj_{tag}{j}", [128, mx]) for j in range(nt)]
        ps = [h.__enter__() for h in hnd]
        ev_cons = {}
        oc = O // 128
        import os as _os2
        if tag == "epi1" and _os2.environ.get("K_EPI_MAXO"):
            oc = min(oc, int(_os2.environ["K_EPI_MAXO"]))
        for o in range(oc):
            w = []
            if o >= 2:
                w.append(ev_cons[(o - 2, nt - 1)])
            if src_bf16:
                g = sc.guard.get(("wbf", o % 2))
                ev_c = sc.dma("sync", WBF[o % 2][:, 0:n_in * 128],
                              colblock_ap(wT, O, o, n_in),
                              waits=tuple(w) + (g,), inc=QF[o % 2])
            else:
                evl = load_wst(o % 2, colblock_ap(wT, O, o, n_in), waits=w,
                               ncols=n_in * 128)
                g = sc.guard.get(("wbf", o % 2))
                ev_c = sc.op("vector", lambda e, o=o: e.tensor_copy(
                    WBF[o % 2][:, 0:n_in * 128], WST[o % 2][:, 0:n_in * 128]),
                    waits=(evl, g), inc=DV)
                wst_used(o % 2, ev_c)
            for tj in range(nt):
                tw = t_widths[tj]
                wmm = [ev_c] + list(wait_rhs)
                if o >= 1:
                    wmm.append(ev_cons[(o - 1, tj)])
                pure = (bias_slice is None and extra_fn is None)
                for i in range(n_in):
                    last = pure and (i == n_in - 1)
                    sc.op("tensor", lambda e, o=o, tj=tj, i=i, tw=tw, last=last: e.matmul(
                        ps[tj][:, 0:tw], WBF[o % 2][:, i * 128:(i + 1) * 128],
                        rhs_fn(i, tj), start=(i == 0), stop=last,
                        skip_group_check=True),
                        waits=wmm if i == 0 else (),
                        inc=PE if last else None)
                if bias_slice is not None:
                    lastb = (extra_fn is None)
                    sc.op("tensor", lambda e, o=o, tj=tj, tw=tw, lastb=lastb: e.matmul(
                        ps[tj][:, 0:tw], bias_slice(o), ONES1[0:1, 0:tw],
                        start=False, stop=lastb, skip_group_check=True),
                        inc=PE if lastb else None)
                if extra_fn is not None:
                    extra_fn(o, tj, ps[tj], tw)
                ev_mm = sc.now(PE)
                ev_cons[(o, tj)] = consume(o, tj, ps[tj], ev_mm)
                sc.guard[("wbf", o % 2)] = ev_mm
        for h in reversed(hnd):
            h.__exit__(None, None, None)
        return ev_cons

    def rope_consume(dst_fn, atab_fn, btab_fn):
        def consume(o, tj, ps_ap, ev_mm):
            ev1 = sc.op("scalar", lambda e: e.activation(
                out=RP[0][:], in_=ps_ap[:, 0:512], func=AF.Copy),
                waits=(ev_mm, sc.rope_ev), inc=AC)
            sc.op("vector", lambda e: e.stream_shuffle(RP[1][:], RP[0][:],
                                                       SWAP_MASK), waits=(ev1,))
            sc.op("vector", lambda e, o=o, tj=tj: e.tensor_mul(
                RP[2][:], RP[0][:], atab_fn(o, tj)))
            sc.op("vector", lambda e, o=o, tj=tj: e.tensor_mul(
                RP[1][:], RP[1][:], btab_fn(o, tj)))
            dst = dst_fn(o, tj)
            if isinstance(dst, tuple):
                ev2 = sc.op("vector", lambda e: e.tensor_add(
                    RP[2][:], RP[2][:], RP[1][:]), inc=DV)
                ev3 = sc.dma("gpsimd", dst[1], RP[2][:], waits=(ev2,), inc=QSR)
                sc.rope_ev = ev3
            else:
                ev2 = sc.op("vector", lambda e, dst=dst: e.tensor_add(
                    dst, RP[2][:], RP[1][:]), inc=DV)
                sc.rope_ev = ev2
            return ev2
        return consume

    def trans_consume(vt_ps, out_fn, key):
        def consume(o, tj, ps_ap, ev_mm):
            ev_c = sc.op("scalar", lambda e: e.activation(
                out=RP[0][:], in_=ps_ap[:, 0:512], func=AF.Copy),
                waits=(ev_mm, sc.rope_ev), inc=AC)
            ev_tp = None
            for k in range(4):
                ev_tp = sc.op("tensor", lambda e, k=k: e.transpose(
                    vt_ps[:, k * 128:(k + 1) * 128],
                    RP[0][:, k * 128:(k + 1) * 128], IDB[:]),
                    waits=(ev_c, sc.guard.get(key)) if k == 0 else (),
                    inc=PE if k == 3 else None)
            ev_ev = sc.op("vector", lambda e: e.tensor_copy(RP[1][:], vt_ps[:]),
                          waits=(ev_tp,), inc=DV)
            sc.guard[key] = ev_ev
            ev_st = sc.dma("gpsimd", out_fn(o, tj),
                           RP[1][:].rearrange("p (k d) -> p k d", k=4),
                           waits=(ev_ev,), inc=QSR)
            sc.rope_ev = ev_st
            return ev_c
        return consume


    def dev_finish():
        for d in range(DC):
            sc.dma("gpsimd", xout[d * 128:(d + 1) * 128, :], WST[0][:],
                   waits=(), inc=QSO[0])
        for s in ALL_G_STORES:
            sc.wait("gpsimd", sc.now(s))
        with nc.Block() as block:
            sc.finalize(block)
        sc.close()
        psum_free()
        for h_ in reversed(list(sb_scoped.values())):
            h_.__exit__(None, None, None)
        sb_scoped.clear()
        for h_ in reversed(ctx):
            h_.__exit__(None, None, None)
        return nc

    sc.op("scalar", lambda e: e.memzero(FSC[0:1, 1:2]))
    ev_id = load_wst(0, ident_i, ncols=128)
    ev_id8 = sc.dma("sync", IDF8[:], ident_i[0:8, 0:8], inc=QMS)
    ev_eg = sc.dma("sync", EGT[:], egate_i, inc=QMS)
    sc.wait("vector", ev_id, ev_eg, ev_id8)
    sc.op("vector", lambda e: e.memset(ONES1[:], 1.0))
    sc.op("vector", lambda e: e.memset(ONESC[:], 1.0))
    sc.op("vector", lambda e: e.memset(ONES1F[:], 1.0))
    ev_idb = sc.op("vector", lambda e: e.tensor_copy(IDB[:], WST[0][:, 0:128]),
                   inc=DV)
    wst_used(0, ev_idb)

    def emit_ln(src_fn, xn_dst, ncols, tag):
        nt = ncols // 512
        sxh = nc.psum_tensor(f"sx_{tag}", [1, 512]); stx = sxh.__enter__()
        sxh2 = nc.psum_tensor(f"sx2_{tag}", [1, 512]); stx2 = sxh2.__enter__()
        pah = nc.psum_tensor(f"la_{tag}", [128, 512]); pa = pah.__enter__()
        pbh = nc.psum_tensor(f"lb_{tag}", [128, 512]); pb = pbh.__enter__()
        M = RS[0:1, 0:512]
        Tt = RS[0:1, 512:1024]
        Sx = RS[0:1, 1024:1536]
        ev_cast = {}
        for d in range(DC):
            evl = load_wst(d % 2, src_fn(d), waits=(ev_cast.get(d - 2),),
                           ncols=ncols)
            ev_cast[d] = sc.op("scalar", lambda e, d=d: e.activation(
                out=xn_dst[:, d * ncols:(d + 1) * ncols],
                in_=WST[d % 2][:, 0:ncols], func=AF.Copy),
                waits=(evl,), inc=AC)
            wst_used(d % 2, ev_cast[d])
        ev_row = None
        ev_apply = None
        ev_sqmm = {}
        for ti in range(nt):
            for d in range(DC):
                sc.op("tensor", lambda e, d=d, ti=ti: e.matmul(
                    stx[0:1, :], ONESC[:],
                    xn_dst[:, d * ncols + ti * 512:d * ncols + (ti + 1) * 512],
                    start=(d == 0), stop=(d == DC - 1), skip_group_check=True),
                    waits=(ev_cast[d], ev_row) if d == 0 else (ev_cast[d],))
                g = (ev_sqmm.get((ti, d - 2)) if d >= 2
                     else ev_sqmm.get((ti - 1, DC - 2 + d)))
                ev_sq = sc.op("scalar", lambda e, d=d, ti=ti: e.activation(
                    out=EVB[d % 2][:],
                    in_=xn_dst[:, d * ncols + ti * 512:d * ncols + (ti + 1) * 512],
                    func=AF.Square), waits=(ev_cast[d], g), inc=AC)
                ev_sqmm[(ti, d)] = sc.op("tensor", lambda e, d=d, ti=ti: e.matmul(
                    stx2[0:1, :], ONESC[:], EVB[d % 2][:],
                    start=(d == 0), stop=(d == DC - 1), skip_group_check=True),
                    waits=(ev_sq,) + ((ev_row,) if d == 0 else ()), inc=PE)
            ev_mm = ev_sqmm[(ti, DC - 1)]
            sc.wait("vector", ev_mm)
            sc.op("vector", lambda e: e.tensor_scalar(
                out=M, in0=stx[0:1, :], scalar1=1.0 / D, scalar2=None,
                op0=ALU.mult), waits=(ev_apply,))
            sc.op("vector", lambda e: e.tensor_scalar(
                out=Tt, in0=stx2[0:1, :], scalar1=1.0 / D, scalar2=None,
                op0=ALU.mult))
            sc.op("vector", lambda e: e.tensor_mul(Sx, M, M))
            sc.op("vector", lambda e: e.tensor_sub(Tt, Tt, Sx))
            ev = sc.op("vector", lambda e: e.tensor_scalar(
                out=Tt, in0=Tt, scalar1=EPS, scalar2=None, op0=ALU.add), inc=DV)
            ev = sc.op("scalar", lambda e: e.sqrt(Tt, Tt), waits=(ev,), inc=AC)
            ev = sc.op("vector", lambda e: e.reciprocal(Tt, Tt), waits=(ev,),
                       inc=DV)
            sc.op("vector", lambda e: e.tensor_mul(M, M, Tt))
            ev_row = sc.op("vector", lambda e: e.tensor_scalar(
                out=M, in0=M, scalar1=-1.0, scalar2=None, op0=ALU.mult), inc=DV)
            ev_a = sc.op("tensor", lambda e: e.matmul(
                pa[:], ONES1F[0:1, :], Tt, start=True, stop=True,
                skip_group_check=True), waits=(ev_row, ev_apply), inc=PE)
            ev_bb = sc.op("tensor", lambda e: e.matmul(
                pb[:], ONES1F[0:1, :], M, start=True, stop=True,
                skip_group_check=True), inc=PE)
            for d in range(DC):
                col = d * ncols + ti * 512
                sc.op("vector", lambda e, col=col: e.tensor_tensor(
                    out=EVB[2][:], in0=xn_dst[:, col:col + 512], in1=pa[:],
                    op=ALU.mult), waits=(ev_a, ev_bb) if d == 0 else ())
                ev_apply = sc.op("vector", lambda e, col=col: e.tensor_tensor(
                    out=xn_dst[:, col:col + 512], in0=EVB[2][:], in1=pb[:],
                    op=ALU.add), inc=DV)
        for h_ in (pbh, pah, sxh2, sxh):
            h_.__exit__(None, None, None)
        return ev_apply

    xn0 = BIGA
    ev_xn0 = emit_ln(lambda d: x0_fm[d * 128:(d + 1) * 128, :], xn0, T, "l0")
    fence()
    xnh = BUF1
    ev_xnh = emit_ln(lambda d: xh_fm[d * 128:(d + 1) * 128, :], xnh, 1024, "lh")
    fence()
    ev_xnh_st = None
    for d in range(DC):
        ev_xnh_st = sc.dma("gpsimd", xnh_scr[d * 128:(d + 1) * 128, :],
                           xnh[:, d * 1024:(d + 1) * 1024],
                           waits=(ev_xnh,), inc=QSH)
    dram_tap("xn0", xn0[:], (ev_xn0,))
    dram_tap("xnh", xnh[:], (ev_xnh,))

    if limit <= 0:
        return dev_finish()
    lgh = nc.psum_tensor("lgps", [128, 8]); lg_ps = lgh.__enter__()
    ev_wr = load_wst(0, bass.AP(wrT.tensor, 0, [[8, 128], [128 * 8, DC], [1, 8]]),
                     ncols=8 * DC)
    ev_wrc = sc.op("vector", lambda e: e.tensor_copy(
        WBF[0][:, 0:8 * DC], WST[0][:, 0:8 * DC]),
        waits=(ev_wr, sc.guard.get(("wbf", 0))), inc=DV)
    wst_used(0, ev_wrc)
    ev_G = {}
    ev_lgprev = None
    for tt in range(16):
        for i in range(DC):
            sc.op("tensor", lambda e, tt=tt, i=i: e.matmul(
                lg_ps[:], xn0[:, i * T + tt * 128:i * T + (tt + 1) * 128],
                WBF[0][:, i * 8:(i + 1) * 8],
                start=(i == 0), stop=(i == DC - 1), skip_group_check=True),
                waits=((ev_wrc, ev_lgprev) if i == 0 else ()),
                inc=PE if i == DC - 1 else None)
        ev_mm = sc.now(PE)
        ev_ev = sc.op("vector", lambda e: e.tensor_copy(SML[:, 0:8], lg_ps[:]),
                      waits=(ev_mm,), inc=DV)
        ev_lgprev = ev_ev
        ev_m8 = sc.op("vector", lambda e: e.max(SML2[:, 0:8], SML[:, 0:8]),
                      waits=(ev_ev,), inc=DV)
        sc.op("vector", lambda e: e.tensor_sub(
            SML2[:, 8:9], SML2[:, 1:2], SML2[:, 0:1]), waits=(ev_m8,))
        ev_d = sc.op("vector", lambda e: e.tensor_sub(
            SML2[:, 9:10], SML2[:, 0:1], SML2[:, 1:2]), inc=DV)
        ev_g2 = sc.op("scalar", lambda e: e.activation(
            out=SML2[:, 10:11], in_=SML2[:, 8:9], func=AF.Sigmoid),
            waits=(ev_d,), inc=AC)
        ev_g1 = sc.op("scalar", lambda e: e.activation(
            out=SML2[:, 11:12], in_=SML2[:, 9:10], func=AF.Sigmoid), inc=AC)
        sc.op("vector", lambda e: e.tensor_scalar(
            out=SML[:, 8:16], in0=SML[:, 0:8], scalar1=SML2[:, 0:1],
            scalar2=None, op0=ALU.is_equal), waits=(ev_g1, ev_g2))
        sc.op("vector", lambda e: e.tensor_scalar(
            out=SML[:, 16:24], in0=SML[:, 0:8], scalar1=SML2[:, 1:2],
            scalar2=None, op0=ALU.is_equal))
        sc.op("vector", lambda e: e.tensor_scalar(
            out=SML[:, 8:16], in0=SML[:, 8:16], scalar1=SML2[:, 11:12],
            scalar2=None, op0=ALU.mult))
        sc.op("vector", lambda e: e.tensor_scalar(
            out=SML[:, 16:24], in0=SML[:, 16:24], scalar1=SML2[:, 10:11],
            scalar2=None, op0=ALU.mult))
        ev_G[tt] = sc.op("vector", lambda e, tt=tt: e.tensor_add(
            GT[:, tt * 8:(tt + 1) * 8], SML[:, 8:16], SML[:, 16:24]), inc=DV)
    dram_tap("G", GT[:], (ev_G[15],))
    sc.guard[("wbf", 0)] = sc.now(PE)
    fence()
    lgh.__exit__(None, None, None)

    eph = [nc.psum_tensor(f"eps{j}", [8, 512]) for j in range(4)]
    eps_ps = [h.__enter__() for h in eph]
    h1, h2 = BUF1, BUF2

    def silu_consume(dst):
        def consume(o, tj, ps_ap, ev_mm, dst=dst):
            return sc.op("scalar", lambda e, o=o, tj=tj: e.activation(
                out=dst[:, o * 1024 + tj * 512:o * 1024 + (tj + 1) * 512],
                in_=ps_ap[:, 0:512], func=AF.Silu), waits=(ev_mm,), inc=AC)
        return consume

    first_g = [True]
    ev_gmm_last = None
    ev_tp_guard = [None]
    prev = (ev_xn0, ev_xnh_st)
    for half in range(2):
        tb = half * 1024

        def rhs_xn(i, tj, tb=tb):
            return xn0[:, i * T + tb + tj * 512:i * T + tb + (tj + 1) * 512]
        ev1 = fm_proj(w1T, INTER, rhs_xn, [512, 512], silu_consume(h1),
                      f"w1{half}", wait_rhs=prev)

        def rhs_h1(i, tj):
            return h1[:, i * 1024 + tj * 512:i * 1024 + (tj + 1) * 512]
        ev3 = fm_proj(w3T, INTER, rhs_h1, [512, 512], silu_consume(h2),
                      f"w3{half}", wait_rhs=(ev1[(DC - 1, 1)],))

        def rhs_h2(i, tj):
            return h2[:, i * 1024 + tj * 512:i * 1024 + (tj + 1) * 512]

        def cons_h(o, tj, ps_ap, ev_mm):
            return sc.op("vector", lambda e, o=o, tj=tj: e.tensor_copy(
                h1[:, o * 1024 + tj * 512:o * 1024 + (tj + 1) * 512],
                ps_ap[:, 0:512]), waits=(ev_mm,), inc=DV)
        evh = fm_proj(w2T, D, rhs_h2, [512, 512], cons_h, f"w2{half}",
                      wait_rhs=(ev3[(DC - 1, 1)],))
        ev_h_all = evh[(DC - 1, 1)]
        tph = nc.psum_tensor(f"tp{half}", [128, 512], BF16)
        tp_ps = tph.__enter__()
        for ttl in range(8):
            tt = half * 8 + ttl
            for d4 in range(4):
                ev_tp = None
                for dd in range(4):
                    d = d4 * 4 + dd
                    ev_tp = sc.op("tensor",
                                  lambda e, d=d, dd=dd, ttl=ttl: e.transpose(
                                      tp_ps[:, dd * 128:(dd + 1) * 128],
                                      h1[:, d * 1024 + ttl * 128:d * 1024 + (ttl + 1) * 128],
                                      IDB[:]),
                                  waits=((ev_h_all, ev_idb, ev_tp_guard[0])
                                         if dd == 0 else ()),
                                  inc=PE if dd == 3 else None)
                ev_ev = sc.op("vector", lambda e, d4=d4: e.tensor_copy(
                    EVB[d4 % 2][:], tp_ps[:]), waits=(ev_tp,), inc=DV)
                ev_tp_guard[0] = ev_ev
                st0 = first_g[0] and True
                if d4 == 0:
                    pass
                last = (tt == 15)
                ev_gmm_last = sc.op("tensor",
                                    lambda e, d4=d4, tt=tt, last=last: e.matmul(
                                        eps_ps[d4][:], GT[:, tt * 8:(tt + 1) * 8],
                                        EVB[d4 % 2][:], start=(tt == 0),
                                        stop=last, skip_group_check=True),
                                    waits=(ev_ev, ev_G[tt]), inc=PE)
        tph.__exit__(None, None, None)
        prev = (evh[(DC - 1, 1)],)
    EPS8 = sbuf_scoped("eps8", [8, 512], F32)
    ev_e_st = None
    for j in range(4):
        ev_eev = sc.op("vector", lambda e, j=j: e.tensor_copy(
            EPS8[0:8, :], eps_ps[j][:]),
            waits=((ev_gmm_last,) if j == 0 else ()) + ((ev_e_st,) if ev_e_st else ()),
            inc=DV)
        ev_e_st = sc.dma("gpsimd", eag_src[0:8, j * 512:(j + 1) * 512],
                         EPS8[0:8, :], waits=(ev_eev,), inc=QSE)
    for h_ in reversed(eph):
        h_.__exit__(None, None, None)
    for s in ALL_G_STORES:
        sc.wait("gpsimd", sc.now(s))
    ev_cc1 = sc.op("gpsimd", lambda e: e.collective_compute(
        "AllReduce", ALU.add, replica_groups=[list(range(8))],
        ins=[eag_src], outs=[eag_dst]), inc=CC1)
    dram_tap("esrc", eag_src, (ev_e_st,))
    sbuf_free("eps8")
    fence()

    if limit <= 1:
        return dev_finish()
    MSK = sbuf_scoped("msk", [128, 4096], BF16)
    ev_y_st = {}
    kv_mark = {}
    for chunk in range(2):
        tb = chunk * 1024
        ev_tabs = []
        for (idx, src, dstc, n) in ((0, qa0[:, tb:tb + 1024], 0, 1024),
                                    (1, qb0[:, tb:tb + 1024], 1024, 1024),
                                    (0, ka0[:, chunk * 1536:(chunk + 1) * 1536], 2048, 1536),
                                    (1, kb0[:, chunk * 1536:(chunk + 1) * 1536], 3584, 1536)):
            evl = load_wst(idx, src, ncols=n)
            evc = sc.op("vector", lambda e, idx=idx, dstc=dstc, n=n: e.tensor_copy(
                TAB[:, dstc:dstc + n], WST[idx][:, 0:n]), waits=(evl,), inc=DV)
            wst_used(idx, evc)
            ev_tabs.append(evc)
        ev_tabs = tuple(ev_tabs)
        dram_tap(f"tab{chunk}", TAB[:, 0:5120], ev_tabs)
        ev_xnh_ld = None
        for d in range(DC):
            ev_xnh_ld = sc.dma("sync", BUF1[:, d * 1024:(d + 1) * 1024],
                               xnh_scr[d * 128:(d + 1) * 128, :],
                               waits=(ev_xnh_st,) if d == 0 else (), inc=QH)

        def qdst(o, tj):
            return BUF2[:, o * 1024 + tj * 512:o * 1024 + (tj + 1) * 512]
        evq = fm_proj(wq0T, D,
                      lambda i, tj, tb=tb: xn0[:, i * T + tb + tj * 512:i * T + tb + (tj + 1) * 512],
                      [512, 512],
                      rope_consume(qdst,
                                   lambda o, tj: TAB[:, tj * 512:(tj + 1) * 512],
                                   lambda o, tj: TAB[:, 1024 + tj * 512:1024 + (tj + 1) * 512]),
                      f"q0c{chunk}", wait_rhs=(ev_xn0,) + ev_tabs)

        def rhs_kv(i, tj, tb=tb, chunk=chunk):
            if tj == 0:
                return BUF1[:, i * 1024 + chunk * 512:i * 1024 + chunk * 512 + 512]
            return xn0[:, i * T + tb + (tj - 1) * 512:i * T + tb + tj * 512]

        def kdst(o, tj, chunk=chunk):
            col = chunk * 1536 + tj * 512
            return ("dma", k0_scr[o * 128:(o + 1) * 128, col:col + 512])
        evk = fm_proj(wk0T, D, rhs_kv, [512, 512, 512],
                      rope_consume(kdst,
                                   lambda o, tj: TAB[:, 2048 + tj * 512:2048 + (tj + 1) * 512],
                                   lambda o, tj: TAB[:, 3584 + tj * 512:3584 + (tj + 1) * 512]),
                      f"k0c{chunk}",
                      wait_rhs=(ev_xnh_ld, evq[(DC - 1, 1)]))

        vth = nc.psum_tensor(f"vt{chunk}", [128, 512], BF16)
        vt_ps = vth.__enter__()

        def vout_fn(o, tj, chunk=chunk):
            row0 = chunk * 1536 + tj * 512
            return bass.AP(v0_scr.tensor, row0 * D + o * 128,
                           [[D, 128], [128 * D, 4], [1, 128]])
        fm_proj(wv0T, D, rhs_kv, [512, 512, 512],
                trans_consume(vt_ps, vout_fn, ("vtp", chunk)),
                f"v0c{chunk}", wait_rhs=(evk[(DC - 1, 2)],))
        vth.__exit__(None, None, None)
        kv_mark[chunk] = g_barrier()
        fence()

        sh = [nc.psum_tensor(f"s{chunk}{j}", [128, 1024]) for j in range(2)]
        s_ps = [h_.__enter__() for h_ in sh]
        dnh = nc.psum_tensor(f"dn{chunk}", [1, 512]); dn_ps = dnh.__enter__()
        yh = nc.psum_tensor(f"yp{chunk}", [128, 512]); y_ps = yh.__enter__()
        rbh = nc.psum_tensor(f"rbp{chunk}", [128, 512]); rb_ps = rbh.__enter__()
        ev_exp = {}
        ev_msk = {}
        ev_av = {}
        ev_recip_prev = None
        ev_rbev_prev = None
        ev_y_prev = None
        ev_mld = {}
        for qg in range(2):
            mwait = ev_msk.get((qg - 1, DC - 1, 3))
            ev_mld[qg] = sc.dma(
                "sync", MSK[:].rearrange("p (j n) -> p j n", j=4),
                bass.AP(maskl0.tensor, (chunk * 8 + qg * 4) * 128 * 1024,
                        [[1024, 128], [128 * 1024, 4], [1, 1024]]),
                waits=(mwait,), inc=QM)
            for h in range(DC):
                hq = qg * DC + h
                bsl = hq % 2
                col0 = chunk * 1536 + qg * 512
                ev_kld = sc.dma(
                    "sync", KWB[bsl][:],
                    k0_scr[h * 128:(h + 1) * 128, col0:col0 + 1024],
                    waits=(kv_mark[chunk], ev_exp.get((hq - 2, 3))), inc=QK[bsl])
                ev_vld = sc.dma(
                    "sync", VWB[bsl][:].rearrange("p (k d) -> p k d", k=8),
                    bass.AP(v0_scr.tensor, col0 * D + h * 128,
                            [[D, 128], [128 * D, 8], [1, 128]]),
                    waits=(kv_mark[chunk], ev_av.get((hq - 2, 3))), inc=QV[bsl])
                for p in range(4):
                    for half in range(2):
                        kt = 2 * p + half
                        if half == 0:
                            if p < 2:
                                w = [ev_kld, ev_exp.get((hq - 1, 2 + p))]
                            else:
                                w = [ev_exp.get((hq, p - 2))]
                        else:
                            w = []
                        sc.op("tensor", lambda e, bsl=bsl, p=p, half=half, kt=kt, h=h, qg=qg: e.matmul(
                            s_ps[p % 2][:, half * 512:(half + 1) * 512],
                            KWB[bsl][:, kt * 128:(kt + 1) * 128],
                            BUF2[:, h * 1024 + qg * 512:h * 1024 + qg * 512 + 512],
                            start=True, stop=True, skip_group_check=True),
                            waits=w, inc=PE if half == 1 else None)
                    ev_s = sc.now(PE)
                    aw = (ev_av.get((hq, p - 2)) if p >= 2
                          else ev_av.get((hq - 1, 2 + p)))
                    ev_exp[(hq, p)] = sc.op(
                        "scalar", lambda e, p=p: e.activation(
                            out=PMB[p % 2][:], in_=s_ps[p % 2][:], func=AF.Exp),
                        waits=(ev_s, aw), inc=AC)
                    ev_msk[(qg, h, p)] = sc.op(
                        "vector", lambda e, p=p: e.tensor_tensor(
                            out=PMB[p % 2][:], in0=PMB[p % 2][:],
                            in1=MSK[:, p * 1024:(p + 1) * 1024], op=ALU.mult),
                        waits=(ev_exp[(hq, p)], ev_mld[qg]), inc=DV)
                    for half in range(2):
                        kt = 2 * p + half
                        sc.op("tensor", lambda e, p=p, half=half: e.matmul(
                            dn_ps[:], ONESC[:],
                            PMB[p % 2][:, half * 512:(half + 1) * 512],
                            start=(p == 0 and half == 0),
                            stop=(p == 3 and half == 1), skip_group_check=True),
                            waits=((ev_msk[(qg, h, p)], ev_recip_prev)
                                   if half == 0 else ()))
                        ev_av[(hq, p)] = sc.op(
                            "tensor", lambda e, p=p, half=half, kt=kt, bsl=bsl: e.matmul(
                                y_ps[:], VWB[bsl][:, kt * 128:(kt + 1) * 128],
                                PMB[p % 2][:, half * 512:(half + 1) * 512],
                                start=(p == 0 and half == 0),
                                stop=(p == 3 and half == 1),
                                skip_group_check=True),
                            waits=((ev_vld, ev_y_prev)
                                   if (p == 0 and half == 0) else ()), inc=PE)
                ev_rc = sc.op("vector", lambda e: e.reciprocal(
                    RS[0:1, 512:1024], dn_ps[:]),
                    waits=(ev_av[(hq, 3)], ev_rbev_prev), inc=DV)
                ev_recip_prev = ev_rc
                ev_rbmm = sc.op("tensor", lambda e: e.matmul(
                    rb_ps[:], ONES1F[0:1, :], RS[0:1, 512:1024],
                    start=True, stop=True, skip_group_check=True),
                    waits=(ev_rc,), inc=PE)
                ev_rbev = sc.op("vector", lambda e, bsl=bsl: e.tensor_copy(
                    RBQ[bsl][:], rb_ps[:]), waits=(ev_rbmm,), inc=DV)
                ev_rbev_prev = ev_rbev
                ev_y = sc.op("vector", lambda e, bsl=bsl: e.tensor_tensor(
                    out=RP[2][:], in0=y_ps[:], in1=RBQ[bsl][:], op=ALU.mult),
                    waits=(ev_rbev, ev_y_st.get((chunk, hq - 1))), inc=DV)
                ev_y_prev = ev_y
                ev_y_st[(chunk, hq)] = sc.dma(
                    "gpsimd",
                    y0_scr[h * 128:(h + 1) * 128, tb + qg * 512:tb + qg * 512 + 512],
                    RP[2][:], waits=(ev_y,), inc=QSY)
        sc.rope_ev = ev_y_st[(chunk, 2 * DC - 1)]
        fence()
        for h_ in [rbh, yh, dnh] + sh[::-1]:
            h_.__exit__(None, None, None)
    y_mark = g_barrier()
    sbuf_free("msk")
    dram_tap("k0", k0_scr, (y_mark,))
    dram_tap("v0", v0_scr, (y_mark,))
    dram_tap("y0", y0_scr, (y_mark,))

    if limit <= 3:
        return dev_finish()
    E8 = sbuf_scoped("e8", [8, 512], F32)
    XT = [sbuf_scoped(f"xt{i}", [128, 512], F32) for i in range(2)]
    ev_yld = None
    for d in range(DC):
        ev_yld = sc.dma("sync", BIGA[:, d * T:(d + 1) * T],
                        y0_scr[d * 128:(d + 1) * 128, :],
                        waits=(y_mark,) if d == 0 else (), inc=QH)
    eth = nc.psum_tensor("etps", [128, 8]); et_ps = eth.__enter__()
    ev_esum = {}
    for qtr in range(4):
        ev_e8 = sc.dma("sync", E8[0:8, 0:512],
                       eag_dst[0:8, qtr * 512:(qtr + 1) * 512],
                       waits=((CC1, 1), ev_esum.get(4 * qtr - 1)), inc=QE)
        for oo in range(4):
            o = qtr * 4 + oo
            ev_t = sc.op("tensor", lambda e, oo=oo: e.transpose(
                et_ps[:], E8[0:8, oo * 128:(oo + 1) * 128], IDF8[:]),
                waits=(ev_e8, ev_esum.get(o - 1)), inc=PE)
            ev_esum[o] = sc.op("vector", lambda e, o=o: e.tensor_scalar(
                out=ESUM[:, o * 8:(o + 1) * 8], in0=et_ps[:], scalar1=EGT[:, 0:1],
                scalar2=None, op0=ALU.mult), waits=(ev_t,), inc=DV)

    ev_x1_st = {}

    def epi0_consume(o, tj, ps_ap, ev_mm):
        b = tj % 2
        prev = ev_x1_st.get((o, tj - 2)) or ev_x1_st.get((o - 1, tj + 2))
        evx = sc.dma("sync", XT[b][:],
                     x0_fm[o * 128:(o + 1) * 128, tj * 512:(tj + 1) * 512],
                     waits=(prev,), inc=QX[b])
        ev = sc.op("vector", lambda e, b=b: e.tensor_tensor(
            out=XT[b][:], in0=ps_ap[:, 0:512], in1=XT[b][:],
            op=ALU.add), waits=(ev_mm, evx), inc=DV)
        if tj == 0:
            ev = sc.op("vector", lambda e, o=o, b=b: e.tensor_tensor(
                out=XT[b][:, 0:8], in0=XT[b][:, 0:8],
                in1=ESUM[:, o * 8:(o + 1) * 8], op=ALU.add),
                waits=(ev_esum[o],), inc=DV)
        ev_st = sc.dma("gpsimd",
                       x1_scr[o * 128:(o + 1) * 128, tj * 512:(tj + 1) * 512],
                       XT[b][:], waits=(ev,), inc=QSX[b])
        ev_x1_st[(o, tj)] = ev_st
        return ev

    fm_proj(wo0T, D,
            lambda i, tj: BIGA[:, i * T + tj * 512:i * T + (tj + 1) * 512],
            [512, 512, 512, 512], epi0_consume, "epi0", wait_rhs=(ev_yld,))
    x1_mark = g_barrier()
    dram_tap("x1", x1_scr, (x1_mark,))
    fence()
    eth.__exit__(None, None, None)
    sbuf_free("xt1", "xt0", "e8")

    if limit <= 4:
        return dev_finish()
    sc.wait("sync", x1_mark)
    xn1 = BIGA
    ev_xn1 = emit_ln(lambda d: x1_scr[d * 128:(d + 1) * 128, :], xn1, T, "l1")
    fence()
    dram_tap("xn1", xn1[:], (ev_xn1,))

    if limit <= 5:
        return dev_finish()
    QABS = sbuf_scoped("qabs_sb", [128, 2048], F32)
    KABS = sbuf_scoped("kabs_sb", [128, 64], F32)
    sc.op("gpsimd", lambda e: e.iota(KABS[:], [[128, 64]], channel_multiplier=1,
                                     allow_small_or_imprecise_dtypes=True))
    ev_bc = None
    for hkv in range(4):
        evl = sc.dma("sync", RS[0:1, 0:512], bq1[0:1, hkv * 512:(hkv + 1) * 512],
                     waits=(ev_bc,), inc=QR)
        sc.wait("vector", evl)
        sc.op("vector", lambda e: e.tensor_add(
            RS[0:1, 512:640], RS[0:1, 0:128], RS[0:1, 128:256]))
        sc.op("vector", lambda e: e.tensor_add(
            RS[0:1, 512:640], RS[0:1, 512:640], RS[0:1, 256:384]))
        ev_s = sc.op("vector", lambda e: e.tensor_add(
            RS[0:1, 512:640], RS[0:1, 512:640], RS[0:1, 384:512]), inc=DV)
        ev_bc = sc.op("scalar", lambda e, hkv=hkv: e.activation(
            out=BIASR[0:1, hkv * 128:(hkv + 1) * 128], in_=RS[0:1, 512:640],
            func=AF.Copy), waits=(ev_s,), inc=AC)
    ev_l = sc.dma("sync", RS[0:1, 0:512], bk1, waits=(ev_bc,), inc=QR)
    ev_bc = sc.op("scalar", lambda e: e.activation(
        out=BIASR[0:1, 512:1024], in_=RS[0:1, 0:512], func=AF.Copy),
        waits=(ev_l,), inc=AC)
    ev_l = sc.dma("sync", RS[0:1, 0:512], bv1, waits=(ev_bc,), inc=QR)
    ev_bc = sc.op("scalar", lambda e: e.activation(
        out=BIASR[0:1, 1024:1536], in_=RS[0:1, 0:512], func=AF.Copy),
        waits=(ev_l,), inc=AC)
    ev_b32 = None
    ev_b64 = None
    for c4 in range(4):
        evl = sc.dma("sync", RS[32:33, 0:512], bo1[0:1, c4 * 512:(c4 + 1) * 512],
                     waits=(ev_b32, ev_bc), inc=QR)
        ev_b32 = sc.op("scalar", lambda e, c4=c4: e.activation(
            out=BIASR[32:33, c4 * 512:(c4 + 1) * 512], in_=RS[32:33, 0:512],
            func=AF.Copy), waits=(evl,), inc=AC)
        evl = sc.dma("sync", RS[64:65, 0:512], bb1[0:1, c4 * 512:(c4 + 1) * 512],
                     waits=(ev_b64, ev_b32), inc=QR)
        ev_b64 = sc.op("scalar", lambda e, c4=c4: e.activation(
            out=BIASR[64:65, c4 * 512:(c4 + 1) * 512], in_=RS[64:65, 0:512],
            func=AF.Copy), waits=(evl,), inc=AC)
    ev_bias = sc.op("scalar", lambda e: e.activation(
        out=FSC[0:1, 4:5], in_=FSC[0:1, 0:1], func=AF.Copy),
        waits=(ev_bc, ev_b32, ev_b64), inc=AC)

    ev_wq_st = None
    for i in range(DC):
        evl = load_wst(i % 2, wq1[i * 128:(i + 1) * 128, :])

        def wv4(off, i=i):
            return bass.AP(WST[i % 2], off, [[2048, 128], [512, 4], [1, 128]])
        sc.wait("vector", evl)
        ev_add = None
        for j in (1, 2, 3):
            ev_add = sc.op("vector", lambda e, wv4=wv4, j=j: e.tensor_add(
                wv4(0), wv4(0), wv4(j * 128)), inc=DV if j == 3 else None)
        ev_cast = sc.op("scalar", lambda e, wv4=wv4: e.activation(
            out=RP[0][:], in_=wv4(0), func=AF.Copy),
            waits=(ev_add, sc.rope_ev), inc=AC)
        wst_used(i % 2, ev_cast)
        ev_wq_st = sc.dma("gpsimd", wqp_scr[i * 128:(i + 1) * 128, :], RP[0][:],
                          waits=(ev_cast, ev_wq_st), inc=QSR)
        sc.rope_ev = ev_wq_st
    wq_mark = g_barrier()
    dram_tap("wqp", wqp_scr, (wq_mark,))

    ev_tl = []
    for idx, tsrc in enumerate((ka1, kb1)):
        evl = load_wst(idx % 2, tsrc)
        evc = sc.op("vector", lambda e, idx=idx: e.tensor_copy(
            TAB[:, idx * 2048:(idx + 1) * 2048], WST[idx % 2][:]),
            waits=(evl,), inc=DV)
        wst_used(idx % 2, evc)
        ev_tl.append(evc)
    ev_tabs1 = tuple(ev_tl)

    def k1dst(o, tj):
        return ("dma", kvag_src[o * 128:(o + 1) * 128, tj * 512:(tj + 1) * 512])
    evk1 = fm_proj(wk1, 512,
                   lambda i, tj: xn1[:, i * T + tj * 512:i * T + (tj + 1) * 512],
                   [512, 512, 512, 512],
                   rope_consume(k1dst,
                                lambda o, tj: TAB[:, tj * 512:(tj + 1) * 512],
                                lambda o, tj: TAB[:, 2048 + tj * 512:2048 + (tj + 1) * 512]),
                   "k1", wait_rhs=(ev_xn1, ev_bias) + ev_tabs1,
                   bias_slice=lambda o: BIASR[0:1, 512 + o * 128:512 + (o + 1) * 128])

    vt1h = nc.psum_tensor("vt1p", [128, 512], BF16)
    vt1_ps = vt1h.__enter__()

    def v1out_fn(o, tj):
        off = 512 * D + (tj * 512) * 512 + o * 128
        return bass.AP(kvag_src.tensor, off, [[512, 128], [128 * 512, 4], [1, 128]])
    evv1 = fm_proj(wv1, 512,
                   lambda i, tj: xn1[:, i * T + tj * 512:i * T + (tj + 1) * 512],
                   [512, 512, 512, 512],
                   trans_consume(vt1_ps, v1out_fn, ("vtp", 9)),
                   "v1", wait_rhs=(evk1[(3, 3)],),
                   bias_slice=lambda o: BIASR[0:1, 1024 + o * 128:1024 + (o + 1) * 128])
    vt1h.__exit__(None, None, None)
    for s in ALL_G_STORES:
        sc.wait("gpsimd", sc.now(s))
    ev_cc2 = sc.op("gpsimd", lambda e: e.collective_compute(
        "AllGather", ALU.bypass,
        replica_groups=[[0, 1, 2, 3], [4, 5, 6, 7]],
        ins=[kvag_src], outs=[kvag_dst]), inc=CC2)
    dram_tap("kvsrc", kvag_src, (sc.now(CC2),))

    sc.wait("sync", wq_mark)
    ev_tlq = []
    for idx, tsrc in enumerate((qa1, qb1)):
        evl = load_wst(idx % 2, tsrc)
        evc = sc.op("vector", lambda e, idx=idx: e.tensor_copy(
            TAB[:, idx * 2048:(idx + 1) * 2048], WST[idx % 2][:]),
            waits=(evl, evk1[(3, 3)]), inc=DV)
        wst_used(idx % 2, evc)
        ev_tlq.append(evc)

    def q1dst(o, tj):
        return BUF1[:, o * 2048 + tj * 512:o * 2048 + (tj + 1) * 512]
    evq1 = fm_proj(wqp_scr, 512,
                   lambda i, tj: xn1[:, i * T + tj * 512:i * T + (tj + 1) * 512],
                   [512, 512, 512, 512],
                   rope_consume(q1dst,
                                lambda o, tj: TAB[:, tj * 512:(tj + 1) * 512],
                                lambda o, tj: TAB[:, 2048 + tj * 512:2048 + (tj + 1) * 512]),
                   "q1", wait_rhs=(evv1[(3, 3)],) + tuple(ev_tlq),
                   bias_slice=lambda o: BIASR[0:1, o * 128:(o + 1) * 128],
                   src_bf16=True)
    ev_q1 = evq1[(3, 3)]
    dram_tap("q1", BUF1[:, 0:8192], (ev_q1,))

    qbh = nc.psum_tensor("qbps", [128, 512]); qb_ps = qbh.__enter__()
    ev_qb_prev = None
    ev_qld_prev = None
    for s in range(4):
        ev_ql = sc.dma("sync", RS[0:1, 0:512], qabs_i[s:s + 1, :],
                       waits=(ev_qld_prev,), inc=QR)
        ev_q = sc.op("tensor", lambda e, s=s: e.matmul(
            qb_ps[:], ONES1F[0:1, :], RS[0:1, 0:512],
            start=True, stop=True, skip_group_check=True),
            waits=(ev_ql, ev_qb_prev), inc=PE)
        ev_qld_prev = ev_q
        ev_qb_prev = sc.op("vector", lambda e, s=s: e.tensor_copy(
            QABS[:, s * 512:(s + 1) * 512], qb_ps[:]), waits=(ev_q,), inc=DV)
    ev_qabs = ev_qb_prev
    qbh.__exit__(None, None, None)
    dram_tap("qabs_t", QABS[:], (ev_qabs,))

    lg2h = nc.psum_tensor("lg2", [128, 8]); lg2_ps = lg2h.__enter__()
    ohh = nc.psum_tensor("ohps", [128, 128], BF16); oh_ps = ohh.__enter__()
    ev_wr2 = load_wst(0, bass.AP(rwT.tensor, 0, [[8, 128], [128 * 8, DC], [1, 8]]),
                      ncols=8 * DC)
    ev_wrc2 = sc.op("vector", lambda e: e.tensor_copy(
        WBF[0][:, 0:8 * DC], WST[0][:, 0:8 * DC]),
        waits=(ev_wr2, sc.guard.get(("wbf", 0))), inc=DV)
    wst_used(0, ev_wrc2)
    ev_mfm = {}
    ev_ohprev = None
    for tt in range(16):
        for i in range(DC):
            sc.op("tensor", lambda e, tt=tt, i=i: e.matmul(
                lg2_ps[:], xn1[:, i * T + tt * 128:i * T + (tt + 1) * 128],
                WBF[0][:, i * 8:(i + 1) * 8],
                start=(i == 0), stop=(i == DC - 1), skip_group_check=True),
                waits=((ev_wrc2, ev_xn1, ev_mfm.get(tt - 1)) if i == 0 else ()),
                inc=PE if i == DC - 1 else None)
        ev_mm = sc.now(PE)
        ev_ev = sc.op("vector", lambda e: e.tensor_copy(SML[:, 0:8], lg2_ps[:]),
                      waits=(ev_mm,), inc=DV)
        ev_m8 = sc.op("vector", lambda e: e.max(SML2[:, 0:8], SML[:, 0:8]),
                      waits=(ev_ev,), inc=DV)
        ev_oh = sc.op("vector", lambda e: e.tensor_scalar(
            out=OHB[:, 0:8], in0=SML[:, 0:8], scalar1=SML2[:, 0:1],
            scalar2=None, op0=ALU.is_equal), waits=(ev_m8, ev_ohprev), inc=DV)
        ev_ex = sc.op("vector", lambda e: e.tensor_copy(
            OHB[:, 0:128].rearrange("p (e r) -> p e r", e=8),
            bass.AP(OHB, 0, [[128, 128], [1, 8], [0, 16]])),
            waits=(ev_oh,), inc=DV)
        ev_tp = sc.op("tensor", lambda e: e.transpose(
            oh_ps[:], OHB[:, 0:128], IDB[:]),
            waits=(ev_ex, ev_mfm.get(tt - 1)), inc=PE)
        ev_mfm[tt] = sc.op("vector", lambda e, tt=tt: e.tensor_copy(
            BUF2[:, 2048 + tt * 128:2048 + (tt + 1) * 128], oh_ps[:]),
            waits=(ev_tp,), inc=DV)
        ev_ohprev = ev_mfm[tt]
    ev_maskfm = ev_mfm[15]
    sc.guard[("wbf", 0)] = sc.now(PE)
    ohh.__exit__(None, None, None)
    lg2h.__exit__(None, None, None)
    dram_tap("maskfm", BUF2[:, 2048:4096], (ev_maskfm,))

    ev_at_l = load_wst(0, colblock_ap(aT, 128, 0), ncols=DC * 128)
    ev_at_c = sc.op("vector", lambda e: e.tensor_copy(
        BUF2[:, 14336:16384], WST[0][:, 0:DC * 128]), waits=(ev_at_l,), inc=DV)
    wst_used(0, ev_at_c)
    hah = nc.psum_tensor("haps", [128, 512]); ha_ps = hah.__enter__()
    ev_hm = {}
    for tj in range(4):
        for i in range(DC):
            sc.op("tensor", lambda e, tj=tj, i=i: e.matmul(
                ha_ps[:], BUF2[:, 14336 + i * 128:14336 + (i + 1) * 128],
                xn1[:, i * T + tj * 512:i * T + (tj + 1) * 512],
                start=(i == 0), stop=(i == DC - 1), skip_group_check=True),
                waits=((ev_at_c, ev_hm.get(tj - 1)) if i == 0 else ()),
                inc=PE if i == DC - 1 else None)
        ev_mm = sc.now(PE)
        ev_hev = sc.op("vector", lambda e, tj=tj: e.tensor_copy(
            EVB[0][:], ha_ps[:]), waits=(ev_mm,), inc=DV)
        ev_hm[tj] = sc.op("vector", lambda e, tj=tj: e.tensor_tensor(
            out=BUF2[:, tj * 512:(tj + 1) * 512], in0=EVB[0][:],
            in1=BUF2[:, 2048 + tj * 512:2048 + (tj + 1) * 512], op=ALU.mult),
            waits=(ev_hev, ev_maskfm), inc=DV)
    ev_hmask = ev_hm[3]
    hah.__exit__(None, None, None)
    dram_tap("hmask", BUF2[:, 0:2048], (ev_hmask,))

    ev_wo1 = None
    for cch in range(4):
        evl = load_wst(cch % 2, wo1[cch * 128:(cch + 1) * 128, :])
        ev_wo1 = sc.op("vector", lambda e, cch=cch: e.tensor_copy(
            BUF2[:, 4096 + cch * 2048:4096 + (cch + 1) * 2048],
            WST[cch % 2][:]), waits=(evl,), inc=DV)
        wst_used(cch % 2, ev_wo1)
    ev_bw_l = load_wst(0, bwT_i)
    ev_bw = sc.op("scalar", lambda e: e.activation(
        out=BUF2[:, 12288:14336], in_=WST[0][:], func=AF.Copy, scale=2.0),
        waits=(ev_bw_l,), inc=AC)
    wst_used(0, ev_bw)
    fence()
    psum_free()

    if limit <= 6:
        return dev_finish()
    sh = [nc.psum_tensor(f"sl1_{j}", [128, 1024]) for j in range(2)]
    s_ps = [h_.__enter__() for h_ in sh]
    dnh = nc.psum_tensor("dnl1", [1, 512]); dn_ps = dnh.__enter__()
    yh = nc.psum_tensor("ypl1", [128, 512]); y_ps = yh.__enter__()
    rbh = nc.psum_tensor("rbl1", [128, 512]); rb_ps = rbh.__enter__()
    KT = [e // 128 for e in ES]

    def kmap(kt):
        kabs = kt * 128
        j = kabs // 1024
        return min(j, 7 - j), kabs % 1024 + (1024 if j > 3 else 0)

    ev_exp = {}
    ev_stt = {}
    ev_av = {}
    ev_recip_prev = None
    ev_rbev_prev = None
    ev_y1_done = {}
    sc.wait("sync", (CC2, 1))
    dram_tap("kvag", kvag_dst, ((CC2, 1),))
    import os as _os
    _only = _os.environ.get("K_L1_ONLY")
    hs = -1
    for h in range(4):
        for s in range(4):
            hs += 1
            if _only is not None and (h, s) != tuple(map(int, _only.split(","))):
                continue
            nkt = KT[s]
            if _os.environ.get("K_L1_MAXKT"):
                nkt = min(nkt, int(_os.environ["K_L1_MAXKT"]))
            npair = nkt // 2
            nblk = nkt // 8 + (1 if nkt % 8 else 0)
            ev_kld = {}
            ev_vld = {}

            def emit_blk_load(blk, hs=hs, h=h):
                rank, col = kmap(blk * 8)
                bsl = blk % 2
                kw = (ev_exp[(hs, (blk - 2) * 4 + 3)] if blk >= 2
                      else ev_exp.get((hs - 1, "last")))
                ev_kld[blk] = sc.dma(
                    "sync", KWB[bsl][:],
                    kvag_dst[rank * 1024 + h * 128:rank * 1024 + (h + 1) * 128,
                             col:col + 1024],
                    waits=(kw,), inc=QK[bsl])
                vw = (ev_av[(hs, (blk - 2) * 8 + 7)] if blk >= 2
                      else ev_av.get((hs - 1, "last")))
                voff = (rank * 1024 + 512) * D + col * 512 + h * 128
                ev_vld[blk] = sc.dma(
                    "sync", VWB[bsl][:].rearrange("p (k d) -> p k d", k=8),
                    bass.AP(kvag_dst.tensor, voff,
                            [[512, 128], [128 * 512, 8], [1, 128]]),
                    waits=(vw,), inc=QV[bsl])

            for blk in range(min(nblk, 2)):
                emit_blk_load(blk)
            for p in range(npair):
                blk = (2 * p) // 8
                bsl = blk % 2
                for half in range(2):
                    kt = 2 * p + half
                    ktl = kt % 8
                    if half == 0:
                        w = [ev_kld[blk]]
                        if p >= 2:
                            w.append(ev_exp.get((hs, p - 2)))
                        else:
                            w.append(ev_stt.get((hs - 1, "last")))
                    else:
                        w = []
                    sc.op("tensor", lambda e, bsl=bsl, half=half, ktl=ktl, h=h, s=s, p=p: e.matmul(
                        s_ps[p % 2][:, half * 512:(half + 1) * 512],
                        KWB[bsl][:, ktl * 128:(ktl + 1) * 128],
                        BUF1[:, h * 2048 + s * 512:h * 2048 + s * 512 + 512],
                        start=True, stop=True, skip_group_check=True),
                        waits=w, inc=PE if half == 1 else None)
                ev_s = sc.now(PE)
                aw = (ev_av.get((hs, 2 * (p - 2) + 1)) if p >= 2
                      else ev_av.get((hs - 1, "last")))
                ev_exp[(hs, p)] = sc.op(
                    "scalar", lambda e, p=p: e.activation(
                        out=PMB[p % 2][:], in_=s_ps[p % 2][:], func=AF.Exp),
                    waits=(ev_s, aw), inc=AC)
                for half in range(2):
                    kt = 2 * p + half
                    ev_stt[(hs, p, half)] = sc.op(
                        "vector", lambda e, p=p, half=half, kt=kt, s=s: e.scalar_tensor_tensor(
                            out=PMB[p % 2][:, half * 512:(half + 1) * 512],
                            in0=QABS[:, s * 512:(s + 1) * 512],
                            scalar=KABS[:, kt:kt + 1],
                            in1=PMB[p % 2][:, half * 512:(half + 1) * 512],
                            op0=ALU.is_ge, op1=ALU.mult),
                        waits=(ev_exp[(hs, p)], ev_qabs), inc=DV)
                ev_stt[(hs, p)] = ev_stt[(hs, p, 1)]
                for half in range(2):
                    kt = 2 * p + half
                    ktl = kt % 8
                    sc.op("tensor", lambda e, p=p, half=half: e.matmul(
                        dn_ps[:], ONESC[:],
                        PMB[p % 2][:, half * 512:(half + 1) * 512],
                        start=(p == 0 and half == 0),
                        stop=(p == npair - 1 and half == 1),
                        skip_group_check=True),
                        waits=(ev_stt[(hs, p, half)],) +
                              ((ev_recip_prev,) if (p == 0 and half == 0) else ()))
                    avw = []
                    if ktl == 0 and half == 0:
                        avw.append(ev_vld[blk])
                    if p == 0 and half == 0:
                        avw.append(ev_y1_done.get(hs - 1))
                    ev_av[(hs, kt)] = sc.op(
                        "tensor", lambda e, p=p, half=half, ktl=ktl, bsl=bsl: e.matmul(
                            y_ps[:], VWB[bsl][:, ktl * 128:(ktl + 1) * 128],
                            PMB[p % 2][:, half * 512:(half + 1) * 512],
                            start=(p == 0 and half == 0),
                            stop=(p == npair - 1 and half == 1),
                            skip_group_check=True),
                        waits=avw, inc=PE)
                if p % 4 == 3 and (p - 3) // 4 + 2 < nblk:
                    emit_blk_load((p - 3) // 4 + 2)
            ev_exp[(hs, "last")] = ev_exp[(hs, npair - 1)]
            ev_stt[(hs, "last")] = ev_stt[(hs, npair - 1)]
            ev_av[(hs, "last")] = ev_av[(hs, nkt - 1)]
            ev_rc = sc.op("vector", lambda e: e.reciprocal(
                RS[0:1, 512:1024], dn_ps[:]),
                waits=(ev_av[(hs, "last")], ev_rbev_prev), inc=DV)
            ev_recip_prev = ev_rc
            ev_rbmm = sc.op("tensor", lambda e: e.matmul(
                rb_ps[:], ONES1F[0:1, :], RS[0:1, 512:1024],
                start=True, stop=True, skip_group_check=True),
                waits=(ev_rc,), inc=PE)
            ev_rbev = sc.op("vector", lambda e, hs=hs: e.tensor_copy(
                RBQ[hs % 2][:], rb_ps[:]), waits=(ev_rbmm,), inc=DV)
            ev_rbev_prev = ev_rbev
            ev_y1_done[hs] = sc.op(
                "vector", lambda e, h=h, s=s, hs=hs: e.tensor_tensor(
                    out=BUF1[:, 8192 + h * 2048 + s * 512:8192 + h * 2048 + (s + 1) * 512],
                    in0=y_ps[:], in1=RBQ[hs % 2][:], op=ALU.mult),
                waits=(ev_rbev,), inc=DV)
    ev_y1 = ev_y1_done[max(ev_y1_done)]
    dram_tap("y1", BUF1[:, 8192:16384], (ev_y1,))
    fence()
    for h_ in [rbh, yh, dnh] + sh[::-1]:
        h_.__exit__(None, None, None)
    sbuf_free("kabs_sb", "qabs_sb")

    if limit <= 7:
        return dev_finish()
    dram_tap("xn1f", xn1[:], ())
    dram_tap("buf2f", BUF2[:], ())
    dram_tap("x1f", x1_scr[0:D, 0:T], ())
    XTE = [sbuf_scoped(f"xte{i}", [128, 512], F32) for i in range(2)]
    BCOL = sbuf_scoped("bcol", [128, 32], F32)
    ev_bo = sc.dma("sync", BCOL[:, 0:16], bocol, inc=QMS)
    ev_bb = sc.dma("sync", BCOL[:, 16:32], bbcol, inc=QMS)
    ev_bcol = sc.op("vector", lambda e: e.tensor_add(
        BCOL[:, 0:16], BCOL[:, 0:16], BCOL[:, 16:32]),
        waits=(ev_bo, ev_bb), inc=DV)
    ev_out_st = {}

    def epi1_extra(o, tj, ps_ap, tw):
        sc.op("tensor", lambda e, o=o, tj=tj: e.matmul(
            ps_ap[:, 0:512], BUF2[:, 12288 + o * 128:12288 + (o + 1) * 128],
            BUF2[:, tj * 512:(tj + 1) * 512],
            start=False, stop=False, skip_group_check=True),
            waits=(ev_hmask, ev_bw) if (o == 0 and tj == 0) else ())
        for cch in range(4):
            sc.op("tensor", lambda e, o=o, tj=tj, cch=cch: e.matmul(
                ps_ap[:, 0:512],
                BUF2[:, 4096 + cch * 2048 + o * 128:4096 + cch * 2048 + (o + 1) * 128],
                BUF1[:, 8192 + cch * 2048 + tj * 512:8192 + cch * 2048 + (tj + 1) * 512],
                start=False, stop=(cch == 3), skip_group_check=True),
                waits=(ev_wo1, ev_y1) if (o == 0 and tj == 0 and cch == 0) else (),
                inc=PE if cch == 3 else None)

    def epi1_consume(o, tj, ps_ap, ev_mm):
        b = tj % 2
        prev = ev_out_st.get((o, tj - 2)) or ev_out_st.get((o - 1, tj + 2))
        evx = sc.dma("sync", XTE[b][:],
                     x1_scr[o * 128:(o + 1) * 128, tj * 512:(tj + 1) * 512],
                     waits=(prev,), inc=QX[b])
        ev = sc.op("vector", lambda e, b=b, o=o, ps_ap=ps_ap: e.scalar_tensor_tensor(
            out=XTE[b][:], in0=ps_ap[:, 0:512], scalar=BCOL[:, o:o + 1],
            in1=XTE[b][:], op0=ALU.add, op1=ALU.add),
            waits=(ev_mm, evx, ev_bcol), inc=DV)
        ev_out_st[(o, tj)] = sc.dma(
            "gpsimd", xout[o * 128:(o + 1) * 128, tj * 512:(tj + 1) * 512],
            XTE[b][:], waits=(ev,), inc=QSO[b])
        return ev

    fm_proj(wbT, D,
            lambda i, tj: xn1[:, i * T + tj * 512:i * T + (tj + 1) * 512],
            [512, 512, 512, 512], epi1_consume, "epi1",
            wait_rhs=(ev_bias,),
            extra_fn=None if _os.environ.get("K_EPI_NOEXTRA") else epi1_extra)
    for s in ALL_G_STORES:
        sc.wait("gpsimd", sc.now(s))

    with nc.Block() as block:
        sc.finalize(block)
    sc.close()
    psum_free()
    for h_ in reversed(list(sb_scoped.values())):
        h_.__exit__(None, None, None)
    sb_scoped.clear()
    for h_ in reversed(ctx):
        h_.__exit__(None, None, None)
    return nc


def kernel(**inputs):
    nc = build_program(DEBUG_TAPS)
    in_maps = [_host_core_inputs(c, inputs) for c in range(NCORE)]
    res = run_bass_kernel_spmd(nc, in_maps, core_ids=list(range(NCORE)))
    out = np.empty((2, 8192, 2048), np.float32)
    for core in range(NCORE):
        b, c = core // 4, core % 4
        pA, pB = c * CH, (7 - c) * CH
        shard = np.asarray(res.results[core]["xout"])
        out[b, pA:pA + CH, :] = shard[:, 0:CH].T
        out[b, pB:pB + CH, :] = shard[:, CH:2 * CH].T
    if DEBUG_TAPS:
        kernel.taps = [{k: np.asarray(v) for k, v in r.items()
                        if k.startswith("tap_")} for r in res.results]
    return out


# version 05
# speedup vs baseline: 1.2430x; 1.2430x over previous
import math
import numpy as np
import ml_dtypes
import concourse.bass as bass
import concourse.mybir as mybir
from concourse.bass_utils import run_bass_kernel_spmd

F32 = mybir.dt.float32
BF16 = mybir.dt.bfloat16
AF = mybir.ActivationFunctionType
ALU = mybir.AluOpType

NCORE = 8
D = 2048
DC = 16
T = 2048
CH = 1024
HALO = 512
INTER = 2048
EPS = 1e-5
SCALE = 1.0 / math.sqrt(128.0)
ES = [3584, 4096, 7680, 8192]
SWAP_MASK = [i ^ 1 for i in range(32)]

DEBUG_TAPS = ()


def _rope_tabs(positions, scale=1.0):
    pos = np.asarray(positions, dtype=np.float64)
    k = np.arange(64, dtype=np.float64)
    invf = 10000.0 ** (-2.0 * k / 128.0)
    th = pos[None, :] * invf[:, None]
    c, s = np.cos(th), np.sin(th)
    A = np.empty((128, len(pos)), np.float64)
    Bt = np.empty((128, len(pos)), np.float64)
    A[0::2] = c; A[1::2] = c
    Bt[0::2] = -s; Bt[1::2] = s
    return (A * scale).astype(np.float32), (Bt * scale).astype(np.float32)


def _l0_masks(pA, pB):
    out = np.zeros((16, 128, 1024), np.float32)
    for ci, base in enumerate((pA, pB)):
        for qg in range(2):
            q0 = base + 512 * qg
            qabs = q0 + np.arange(512)
            for pr in range(4):
                for half in range(2):
                    kt = 2 * pr + half
                    kabs = (q0 - 512 + kt * 128) + np.arange(128)
                    valid = ((kabs[:, None] <= qabs[None, :])
                             & (kabs[:, None] > qabs[None, :] - 512)
                             & (kabs[:, None] >= 0))
                    out[ci * 8 + qg * 4 + pr, :, half * 512:(half + 1) * 512] = valid
    return out.astype(ml_dtypes.bfloat16)


def _host_core_inputs(core, ins):
    b, c = core // 4, core % 4
    pA, pB = c * CH, (7 - c) * CH
    x = np.asarray(ins["L0_attn_x"])
    x0_fm = np.ascontiguousarray(
        np.concatenate([x[b, pA:pA + CH], x[b, pB:pB + CH]], 0).T)
    hA = np.zeros((HALO, D), np.float32) if pA == 0 else x[b, pA - HALO:pA]
    hB = x[b, pB - HALO:pB]
    xh_fm = np.ascontiguousarray(np.concatenate([hA, hB], 0).T)

    own_pos = np.concatenate([pA + np.arange(CH), pB + np.arange(CH)])
    kposA = np.concatenate([np.maximum(pA - HALO + np.arange(HALO), 0),
                            pA + np.arange(CH)])
    kposB = np.concatenate([pB - HALO + np.arange(HALO), pB + np.arange(CH)])
    qa0, qb0 = _rope_tabs(own_pos, SCALE)
    ka0A, kb0A = _rope_tabs(kposA)
    ka0B, kb0B = _rope_tabs(kposB)
    qa1, qb1 = _rope_tabs(own_pos, SCALE)
    ka1, kb1 = _rope_tabs(own_pos)

    qabs = np.stack([pA + np.arange(512), pA + 512 + np.arange(512),
                     pB + np.arange(512), pB + 512 + np.arange(512)]
                    ).astype(np.float32)

    cat = np.concatenate
    return {
        "x0_fm": x0_fm, "xh_fm": xh_fm,
        "qa0": qa0, "qb0": qb0,
        "ka0": cat([ka0A, ka0B], 1), "kb0": cat([kb0A, kb0B], 1),
        "qa1": qa1, "qb1": qb1, "ka1": ka1, "kb1": kb1,
        "maskl0": _l0_masks(pA, pB),
        "qabs": qabs,
        "ident": np.eye(128, dtype=np.float32),
        "egate": np.full((128, 1), 1.0 if core == 0 else 0.0, np.float32),
        "wq0T": np.ascontiguousarray(np.asarray(ins["L0_attn_Wq"]).T),
        "wk0T": np.ascontiguousarray(np.asarray(ins["L0_attn_Wk"]).T),
        "wv0T": np.ascontiguousarray(np.asarray(ins["L0_attn_Wv"]).T),
        "wo0T": np.ascontiguousarray(np.asarray(ins["L0_attn_Wo"]).T),
        "w1T": np.ascontiguousarray(np.asarray(ins["L0_ffn_w1"]).T),
        "w3T": np.ascontiguousarray(np.asarray(ins["L0_ffn_w3"]).T),
        "w2T": np.ascontiguousarray(np.asarray(ins["L0_ffn_w2"]).T),
        "wrT": np.ascontiguousarray(np.asarray(ins["L0_ffn_wr"]).T),
        "wq1": np.ascontiguousarray(np.asarray(ins["L1_attn_Wq"])),
        "bq1": np.asarray(ins["L1_attn_bq"]).reshape(1, -1).copy(),
        "wk1": np.ascontiguousarray(np.asarray(ins["L1_attn_Wk"])),
        "bk1": np.asarray(ins["L1_attn_bk"]).reshape(1, -1).copy(),
        "wv1": np.ascontiguousarray(np.asarray(ins["L1_attn_Wv"])),
        "bv1": np.asarray(ins["L1_attn_bv"]).reshape(1, -1).copy(),
        "wo1": np.ascontiguousarray(np.asarray(ins["L1_attn_Wo"])),
        "bo1": np.asarray(ins["L1_attn_bo"]).reshape(1, -1).copy(),
        "bocol": np.ascontiguousarray(
            np.asarray(ins["L1_attn_bo"]).reshape(16, 128).T),
        "bbcol": np.ascontiguousarray(
            np.asarray(ins["L1_ffn_b_base"]).reshape(16, 128).T),
        "wbT": np.ascontiguousarray(np.asarray(ins["L1_ffn_W_base"]).T),
        "bb1": np.asarray(ins["L1_ffn_b_base"]).reshape(1, -1).copy(),
        "rwT": np.ascontiguousarray(np.asarray(ins["L1_ffn_router_W"]).T),
        "aT": np.ascontiguousarray(
            np.asarray(ins["L1_ffn_A"]).transpose(2, 0, 1).reshape(D, 128)),
        "bwT": np.ascontiguousarray(
            np.asarray(ins["L1_ffn_Bw"]).transpose(0, 2, 1).reshape(128, D)),
    }


ENGS = ("sync", "scalar", "vector", "tensor", "gpsimd")


class Sched:
    def __init__(self, nc):
        self.nc = nc
        self.ops = {e: [] for e in ENGS}
        self._ctx = []
        self.sems = {}
        self.cnt = {}
        self.guard = {}
        self.rope_ev = None
        self.autochain = {}
        self.phase = "init"

    def sem(self, name):
        h = self.nc.semaphore(name)
        self.sems[name] = h.__enter__()
        self._ctx.append(h)
        self.cnt[name] = 0
        return name

    def op(self, eng, emit, waits=(), inc=None, inc_by=1):
        waits = tuple(w for w in waits if w is not None)
        auto = self.autochain.get(eng)
        if auto is not None and emit is not None:
            prev = self.cnt[auto]
            if prev > 0:
                waits = waits + ((auto, prev),)
            if inc is None:
                inc = auto
        ev = None
        if inc is not None:
            self.cnt[inc] += inc_by
            ev = (inc, self.cnt[inc])
        self.ops[eng].append((waits, emit, (inc, inc_by) if inc else None,
                              self.phase))
        return ev

    def dma(self, eng, out, in_, waits=(), inc=None, **kw):
        return self.op(eng, lambda e: e.dma_start(out=out, in_=in_, **kw),
                       waits, inc, 16)

    def wait(self, eng, *evs):
        evs = tuple(e for e in evs if e is not None)
        if evs:
            self.ops[eng].append((evs, None, None, self.phase))

    def now(self, sem):
        return (sem, self.cnt[sem])

    def finalize(self, block):
        for eng_name in ENGS:
            my_ops = self.ops[eng_name]

            def run(eng, _ops=my_ops):
                for waits, emit, inc, ph in _ops:
                    for (s, v) in waits:
                        eng.wait_ge(self.sems[s], v)
                    if emit is None:
                        assert inc is None
                        continue
                    with self.nc.named_scope(ph):
                        inst = emit(eng)
                    if inc is not None:
                        inst.then_inc(self.sems[inc[0]], inc[1])
            getattr(block, eng_name)(run)

    def close(self):
        for h in reversed(self._ctx):
            h.__exit__(None, None, None)


def build_program(debug_taps=(), limit=99):
    nc = bass.Bass("TRN2", target_bir_lowering=False)

    def din(name, shape, dt=F32):
        return nc.dram_tensor(name, shape, dt, kind="ExternalInput").ap()

    x0_fm = din("x0_fm", [D, T]);      xh_fm = din("xh_fm", [D, 1024])
    qa0 = din("qa0", [128, T]);        qb0 = din("qb0", [128, T])
    ka0 = din("ka0", [128, 3072]);     kb0 = din("kb0", [128, 3072])
    qa1 = din("qa1", [128, T]);        qb1 = din("qb1", [128, T])
    ka1 = din("ka1", [128, T]);        kb1 = din("kb1", [128, T])
    maskl0 = din("maskl0", [16, 128, 1024], BF16)
    qabs_i = din("qabs", [4, 512])
    ident_i = din("ident", [128, 128])
    egate_i = din("egate", [128, 1])
    wq0T = din("wq0T", [D, D]); wk0T = din("wk0T", [D, D])
    wv0T = din("wv0T", [D, D]); wo0T = din("wo0T", [D, D])
    w1T = din("w1T", [D, INTER]); w3T = din("w3T", [INTER, INTER])
    w2T = din("w2T", [INTER, D]); wrT = din("wrT", [D, 8])
    wq1 = din("wq1", [D, D]); bq1 = din("bq1", [1, D])
    wk1 = din("wk1", [D, 512]); bk1 = din("bk1", [1, 512])
    wv1 = din("wv1", [D, 512]); bv1 = din("bv1", [1, 512])
    wo1 = din("wo1", [512, D]); bo1 = din("bo1", [1, D])
    wbT = din("wbT", [D, D]); bb1 = din("bb1", [1, D])
    bocol = din("bocol", [128, 16]); bbcol = din("bbcol", [128, 16])
    rwT = din("rwT", [D, 8])
    aT = din("aT", [D, 128])
    bwT_i = din("bwT", [128, D])

    xout = nc.dram_tensor("xout", [D, T], F32, kind="ExternalOutput").ap()

    xnh_scr = nc.dram_tensor("xnh_scr", [D, 1024], BF16).ap()
    k0_scr = nc.dram_tensor("k0_scr", [D, 3072], BF16).ap()
    v0_scr = nc.dram_tensor("v0_scr", [3072, D], BF16).ap()
    y0_scr = nc.dram_tensor("y0_scr", [D, T], BF16).ap()
    x1_scr = nc.dram_tensor("x1_scr", [D, T], F32).ap()
    wqp_scr = nc.dram_tensor("wqp_scr", [D, 512], BF16).ap()
    kvag_src = nc.dram_tensor("kvag_src", [1024, D], BF16).ap()
    kvag_dst = nc.dram_tensor("kvag_dst", [4096, D], BF16).ap()
    eag_src = nc.dram_tensor("eag_src", [8, D], F32).ap()
    eag_dst = nc.dram_tensor("eag_dst", [8, D], F32).ap()

    ctx = []

    def sbuf(name, shape, dt):
        h = nc.sbuf_tensor(name, shape, dt)
        t = h.__enter__(); ctx.append(h)
        return t

    BIGA = sbuf("biga", [128, DC * T], BF16)
    BUF1 = sbuf("buf1", [128, DC * 1024], BF16)
    BUF2 = sbuf("buf2", [128, DC * 1024], BF16)
    WST = [sbuf(f"wst{i}", [128, T], F32) for i in range(2)]
    WBF = [sbuf(f"wbf{i}", [128, T], BF16) for i in range(2)]
    PMB = [sbuf(f"pmb{i}", [128, 1024], BF16) for i in range(2)]
    KWB = [sbuf(f"kwb{i}", [128, 1024], BF16) for i in range(2)]
    VWB = [sbuf(f"vwb{i}", [128, 1024], BF16) for i in range(2)]
    TAB = sbuf("tab", [128, 5120], BF16)
    RS = sbuf("rs", [65, 1536], F32)
    BIASR = sbuf("biasr", [65, 2048], BF16)
    ONES1 = sbuf("ones1", [65, 512], BF16)
    ONESC = sbuf("onesc", [128, 1], BF16)
    ONES1F = sbuf("ones1f", [65, 128], F32)
    IDB = sbuf("idb", [128, 128], BF16)
    IDF8 = sbuf("idf8", [8, 8], F32)
    EGT = sbuf("egt", [128, 1], F32)
    SML = sbuf("sml", [128, 64], F32)
    SML2 = sbuf("sml2", [128, 64], F32)
    GT = sbuf("gt", [128, 16 * 8], BF16)
    OHB = sbuf("ohb", [128, 128], BF16)
    RBQ = [sbuf(f"rbq{i}", [128, 512], F32) for i in range(2)]
    EVB = [sbuf(f"evb{i}", [128, 512], BF16) for i in range(3)]
    RP = [sbuf(f"rp{i}", [128, 512], BF16) for i in range(3)]
    ESUM = sbuf("esum", [128, 128], F32)
    FSC = sbuf("fsc", [1, 8], F32)

    sb_scoped = {}

    def sbuf_scoped(name, shape, dt):
        h = nc.sbuf_tensor(name, shape, dt)
        t = h.__enter__(); sb_scoped[name] = h
        return t

    def sbuf_free(*names):
        for n in names:
            sb_scoped.pop(n).__exit__(None, None, None)

    sc = Sched(nc)
    PE = sc.sem("pe"); DV = sc.sem("dv"); AC = sc.sem("ac"); GP = sc.sem("gp")
    CC1 = sc.sem("cc1"); CC2 = sc.sem("cc2")
    sc.autochain = {"vector": DV, "scalar": AC}
    QW = [sc.sem("qw0"), sc.sem("qw1")]
    QF = [sc.sem("qf0"), sc.sem("qf1")]
    QK = [sc.sem("qk0"), sc.sem("qk1")]
    QV = [sc.sem("qv0"), sc.sem("qv1")]
    QM = sc.sem("qm")
    QH = sc.sem("qh")
    QE = sc.sem("qe")
    QR = sc.sem("qr")
    QX = [sc.sem("qx0"), sc.sem("qx1")]
    QMS = sc.sem("qms")
    QSR = sc.sem("qsr")
    QSY = sc.sem("qsy")
    QSX = [sc.sem("qsx0"), sc.sem("qsx1")]
    QSO = [sc.sem("qso0"), sc.sem("qso1")]
    QSH = sc.sem("qsh")
    QSE = sc.sem("qse")
    ALL_G_STORES = (QSR, QSY, QSX[0], QSX[1], QSO[0], QSO[1], QSH, QSE)

    def g_barrier():
        for s in ALL_G_STORES:
            sc.wait("gpsimd", sc.now(s))
        return sc.op("gpsimd", lambda e: e.memset(FSC[0:1, 3:4], 0.0), inc=GP)
    FSEM = {e: sc.sem("f_" + e) for e in ("scalar", "vector", "tensor", "gpsimd")}

    def fence():
        evs = []
        evs.append(sc.op("vector", lambda e: e.memset(FSC[0:1, 0:1], 0.0),
                         inc=FSEM["vector"]))
        evs.append(sc.op("scalar", lambda e: e.activation(
            out=FSC[0:1, 1:2], in_=FSC[0:1, 1:2], func=AF.Copy),
            inc=FSEM["scalar"]))
        evs.append(sc.op("tensor", lambda e: e.nop(), inc=FSEM["tensor"]))
        evs.append(sc.op("gpsimd", lambda e: e.memset(FSC[0:1, 2:3], 0.0),
                         inc=FSEM["gpsimd"]))
        names = ("f_vector", "f_scalar", "f_tensor", "f_gpsimd")
        for e in ("vector", "scalar", "tensor", "gpsimd", "sync"):
            sc.wait(e, *[ev for ev in evs if ev[0] != "f_" + e])

    psum_live = []

    def psum(name, shape, dt=F32):
        h = nc.psum_tensor(name, shape, dt)
        t = h.__enter__(); psum_live.append(h)
        return t

    def psum_free():
        for h in reversed(psum_live):
            h.__exit__(None, None, None)
        psum_live.clear()

    def dram_tap(name, src_ap, waits):
        if name in debug_taps:
            shp = list(src_ap.shape)
            t = nc.dram_tensor("tap_" + name, shp, src_ap.dtype,
                               kind="ExternalOutput").ap()
            sc.dma("gpsimd", t, src_ap, waits=waits, inc=QSE)

    def load_wst(slot, src_ap, waits=(), ncols=None):
        g = sc.guard.get(("wst", slot))
        ev = sc.dma("sync", WST[slot][:, 0:ncols] if ncols else WST[slot][:],
                    src_ap, waits=tuple(waits) + (g,), inc=QW[slot])
        return ev

    def wst_used(slot, ev):
        sc.guard[("wst", slot)] = ev

    def colblock_ap(wT, O, o, nin=DC):
        return bass.AP(wT.tensor, o * 128, [[O, 128], [128 * O, nin], [1, 128]])

    def fm_proj(wT, O, rhs_fn, t_widths, consume, tag, wait_rhs=(),
                bias_slice=None, extra_fn=None, n_in=DC, src_bf16=False):
        nt = len(t_widths)
        assert nt <= 4
        mx = max(t_widths)
        hnd = [nc.psum_tensor(f"pj_{tag}{j}", [128, mx]) for j in range(nt)]
        ps = [h.__enter__() for h in hnd]
        ev_cons = {}
        oc = O // 128
        import os as _os2
        if tag == "epi1" and _os2.environ.get("K_EPI_MAXO"):
            oc = min(oc, int(_os2.environ["K_EPI_MAXO"]))
        for o in range(oc):
            w = []
            if o >= 2:
                w.append(ev_cons[(o - 2, nt - 1)])
            if src_bf16:
                g = sc.guard.get(("wbf", o % 2))
                ev_c = sc.dma("sync", WBF[o % 2][:, 0:n_in * 128],
                              colblock_ap(wT, O, o, n_in),
                              waits=tuple(w) + (g,), inc=QF[o % 2])
            else:
                evl = load_wst(o % 2, colblock_ap(wT, O, o, n_in), waits=w,
                               ncols=n_in * 128)
                g = sc.guard.get(("wbf", o % 2))
                ev_c = sc.op("vector", lambda e, o=o: e.tensor_copy(
                    WBF[o % 2][:, 0:n_in * 128], WST[o % 2][:, 0:n_in * 128]),
                    waits=(evl, g), inc=DV)
                wst_used(o % 2, ev_c)
            for tj in range(nt):
                tw = t_widths[tj]
                wmm = [ev_c] + list(wait_rhs)
                if o >= 1:
                    wmm.append(ev_cons[(o - 1, tj)])
                pure = (bias_slice is None and extra_fn is None)
                for i in range(n_in):
                    last = pure and (i == n_in - 1)
                    sc.op("tensor", lambda e, o=o, tj=tj, i=i, tw=tw, last=last: e.matmul(
                        ps[tj][:, 0:tw], WBF[o % 2][:, i * 128:(i + 1) * 128],
                        rhs_fn(i, tj), start=(i == 0), stop=last,
                        skip_group_check=True),
                        waits=wmm if i == 0 else (),
                        inc=PE if last else None)
                if bias_slice is not None:
                    lastb = (extra_fn is None)
                    sc.op("tensor", lambda e, o=o, tj=tj, tw=tw, lastb=lastb: e.matmul(
                        ps[tj][:, 0:tw], bias_slice(o), ONES1[0:1, 0:tw],
                        start=False, stop=lastb, skip_group_check=True),
                        inc=PE if lastb else None)
                if extra_fn is not None:
                    extra_fn(o, tj, ps[tj], tw)
                ev_mm = sc.now(PE)
                ev_cons[(o, tj)] = consume(o, tj, ps[tj], ev_mm)
                sc.guard[("wbf", o % 2)] = ev_mm
        for h in reversed(hnd):
            h.__exit__(None, None, None)
        return ev_cons

    def rope_consume(dst_fn, atab_fn, btab_fn):
        def consume(o, tj, ps_ap, ev_mm):
            ev1 = sc.op("scalar", lambda e: e.activation(
                out=RP[0][:], in_=ps_ap[:, 0:512], func=AF.Copy),
                waits=(ev_mm, sc.rope_ev), inc=AC)
            sc.op("vector", lambda e: e.stream_shuffle(RP[1][:], RP[0][:],
                                                       SWAP_MASK), waits=(ev1,))
            sc.op("vector", lambda e, o=o, tj=tj: e.tensor_mul(
                RP[2][:], RP[0][:], atab_fn(o, tj)))
            sc.op("vector", lambda e, o=o, tj=tj: e.tensor_mul(
                RP[1][:], RP[1][:], btab_fn(o, tj)))
            dst = dst_fn(o, tj)
            if isinstance(dst, tuple):
                ev2 = sc.op("vector", lambda e: e.tensor_add(
                    RP[2][:], RP[2][:], RP[1][:]), inc=DV)
                ev3 = sc.dma("gpsimd", dst[1], RP[2][:], waits=(ev2,), inc=QSR)
                sc.rope_ev = ev3
            else:
                ev2 = sc.op("vector", lambda e, dst=dst: e.tensor_add(
                    dst, RP[2][:], RP[1][:]), inc=DV)
                sc.rope_ev = ev2
            return ev2
        return consume

    def trans_consume(vt_ps, out_fn, key):
        def consume(o, tj, ps_ap, ev_mm):
            ev_c = sc.op("scalar", lambda e: e.activation(
                out=RP[0][:], in_=ps_ap[:, 0:512], func=AF.Copy),
                waits=(ev_mm, sc.rope_ev), inc=AC)
            ev_tp = None
            for k in range(4):
                ev_tp = sc.op("tensor", lambda e, k=k: e.transpose(
                    vt_ps[:, k * 128:(k + 1) * 128],
                    RP[0][:, k * 128:(k + 1) * 128], IDB[:]),
                    waits=(ev_c, sc.guard.get(key)) if k == 0 else (),
                    inc=PE if k == 3 else None)
            ev_ev = sc.op("vector", lambda e: e.tensor_copy(RP[1][:], vt_ps[:]),
                          waits=(ev_tp,), inc=DV)
            sc.guard[key] = ev_ev
            ev_st = sc.dma("gpsimd", out_fn(o, tj),
                           RP[1][:].rearrange("p (k d) -> p k d", k=4),
                           waits=(ev_ev,), inc=QSR)
            sc.rope_ev = ev_st
            return ev_c
        return consume


    def dev_finish():
        for d in range(DC):
            sc.dma("gpsimd", xout[d * 128:(d + 1) * 128, :], WST[0][:],
                   waits=(), inc=QSO[0])
        for s in ALL_G_STORES:
            sc.wait("gpsimd", sc.now(s))
        with nc.Block() as block:
            sc.finalize(block)
        sc.close()
        psum_free()
        for h_ in reversed(list(sb_scoped.values())):
            h_.__exit__(None, None, None)
        sb_scoped.clear()
        for h_ in reversed(ctx):
            h_.__exit__(None, None, None)
        return nc

    sc.phase = "p0_ln0"
    sc.op("scalar", lambda e: e.memzero(FSC[0:1, 1:2]))
    ev_id = load_wst(0, ident_i, ncols=128)
    ev_id8 = sc.dma("sync", IDF8[:], ident_i[0:8, 0:8], inc=QMS)
    ev_eg = sc.dma("sync", EGT[:], egate_i, inc=QMS)
    sc.wait("vector", ev_id, ev_eg, ev_id8)
    sc.op("vector", lambda e: e.memset(ONES1[:], 1.0))
    sc.op("vector", lambda e: e.memset(ONESC[:], 1.0))
    sc.op("vector", lambda e: e.memset(ONES1F[:], 1.0))
    ev_idb = sc.op("vector", lambda e: e.tensor_copy(IDB[:], WST[0][:, 0:128]),
                   inc=DV)
    wst_used(0, ev_idb)

    def emit_ln(src_fn, xn_dst, ncols, tag):
        nt = ncols // 512
        sxh = nc.psum_tensor(f"sx_{tag}", [1, 512]); stx = sxh.__enter__()
        sxh2 = nc.psum_tensor(f"sx2_{tag}", [1, 512]); stx2 = sxh2.__enter__()
        pah = nc.psum_tensor(f"la_{tag}", [128, 512]); pa = pah.__enter__()
        pbh = nc.psum_tensor(f"lb_{tag}", [128, 512]); pb = pbh.__enter__()
        M = RS[0:1, 0:512]
        Tt = RS[0:1, 512:1024]
        Sx = RS[0:1, 1024:1536]
        ev_cast = {}
        for d in range(DC):
            evl = load_wst(d % 2, src_fn(d), waits=(ev_cast.get(d - 2),),
                           ncols=ncols)
            ev_cast[d] = sc.op("scalar", lambda e, d=d: e.activation(
                out=xn_dst[:, d * ncols:(d + 1) * ncols],
                in_=WST[d % 2][:, 0:ncols], func=AF.Copy),
                waits=(evl,), inc=AC)
            wst_used(d % 2, ev_cast[d])
        ev_row = None
        ev_apply = None
        ev_sqmm = {}
        for ti in range(nt):
            for d in range(DC):
                sc.op("tensor", lambda e, d=d, ti=ti: e.matmul(
                    stx[0:1, :], ONESC[:],
                    xn_dst[:, d * ncols + ti * 512:d * ncols + (ti + 1) * 512],
                    start=(d == 0), stop=(d == DC - 1), skip_group_check=True),
                    waits=(ev_cast[d], ev_row) if d == 0 else (ev_cast[d],))
                g = (ev_sqmm.get((ti, d - 2)) if d >= 2
                     else ev_sqmm.get((ti - 1, DC - 2 + d)))
                ev_sq = sc.op("scalar", lambda e, d=d, ti=ti: e.activation(
                    out=EVB[d % 2][:],
                    in_=xn_dst[:, d * ncols + ti * 512:d * ncols + (ti + 1) * 512],
                    func=AF.Square), waits=(ev_cast[d], g), inc=AC)
                ev_sqmm[(ti, d)] = sc.op("tensor", lambda e, d=d, ti=ti: e.matmul(
                    stx2[0:1, :], ONESC[:], EVB[d % 2][:],
                    start=(d == 0), stop=(d == DC - 1), skip_group_check=True),
                    waits=(ev_sq,) + ((ev_row,) if d == 0 else ()), inc=PE)
            ev_mm = ev_sqmm[(ti, DC - 1)]
            sc.wait("vector", ev_mm)
            sc.op("vector", lambda e: e.tensor_scalar(
                out=M, in0=stx[0:1, :], scalar1=1.0 / D, scalar2=None,
                op0=ALU.mult), waits=(ev_apply,))
            sc.op("vector", lambda e: e.tensor_scalar(
                out=Tt, in0=stx2[0:1, :], scalar1=1.0 / D, scalar2=None,
                op0=ALU.mult))
            sc.op("vector", lambda e: e.tensor_mul(Sx, M, M))
            sc.op("vector", lambda e: e.tensor_sub(Tt, Tt, Sx))
            ev = sc.op("vector", lambda e: e.tensor_scalar(
                out=Tt, in0=Tt, scalar1=EPS, scalar2=None, op0=ALU.add), inc=DV)
            ev = sc.op("scalar", lambda e: e.sqrt(Tt, Tt), waits=(ev,), inc=AC)
            ev = sc.op("vector", lambda e: e.reciprocal(Tt, Tt), waits=(ev,),
                       inc=DV)
            sc.op("vector", lambda e: e.tensor_mul(M, M, Tt))
            ev_row = sc.op("vector", lambda e: e.tensor_scalar(
                out=M, in0=M, scalar1=-1.0, scalar2=None, op0=ALU.mult), inc=DV)
            ev_a = sc.op("tensor", lambda e: e.matmul(
                pa[:], ONES1F[0:1, :], Tt, start=True, stop=True,
                skip_group_check=True), waits=(ev_row, ev_apply), inc=PE)
            ev_bb = sc.op("tensor", lambda e: e.matmul(
                pb[:], ONES1F[0:1, :], M, start=True, stop=True,
                skip_group_check=True), inc=PE)
            for d in range(DC):
                col = d * ncols + ti * 512
                sc.op("vector", lambda e, col=col: e.tensor_tensor(
                    out=EVB[2][:], in0=xn_dst[:, col:col + 512], in1=pa[:],
                    op=ALU.mult), waits=(ev_a, ev_bb) if d == 0 else ())
                ev_apply = sc.op("vector", lambda e, col=col: e.tensor_tensor(
                    out=xn_dst[:, col:col + 512], in0=EVB[2][:], in1=pb[:],
                    op=ALU.add), inc=DV)
        for h_ in (pbh, pah, sxh2, sxh):
            h_.__exit__(None, None, None)
        return ev_apply

    xn0 = BIGA
    ev_xn0 = emit_ln(lambda d: x0_fm[d * 128:(d + 1) * 128, :], xn0, T, "l0")
    fence()
    xnh = BUF1
    ev_xnh = emit_ln(lambda d: xh_fm[d * 128:(d + 1) * 128, :], xnh, 1024, "lh")
    fence()
    ev_xnh_st = None
    for d in range(DC):
        ev_xnh_st = sc.dma("gpsimd", xnh_scr[d * 128:(d + 1) * 128, :],
                           xnh[:, d * 1024:(d + 1) * 1024],
                           waits=(ev_xnh,), inc=QSH)
    dram_tap("xn0", xn0[:], (ev_xn0,))
    dram_tap("xnh", xnh[:], (ev_xnh,))

    if limit <= 0:
        return dev_finish()
    sc.phase = "p1_l0ffn"
    lgh = nc.psum_tensor("lgps", [128, 8]); lg_ps = lgh.__enter__()
    ev_wr = load_wst(0, bass.AP(wrT.tensor, 0, [[8, 128], [128 * 8, DC], [1, 8]]),
                     ncols=8 * DC)
    ev_wrc = sc.op("vector", lambda e: e.tensor_copy(
        WBF[0][:, 0:8 * DC], WST[0][:, 0:8 * DC]),
        waits=(ev_wr, sc.guard.get(("wbf", 0))), inc=DV)
    wst_used(0, ev_wrc)
    ev_G = {}
    ev_lgprev = None
    for tt in range(16):
        for i in range(DC):
            sc.op("tensor", lambda e, tt=tt, i=i: e.matmul(
                lg_ps[:], xn0[:, i * T + tt * 128:i * T + (tt + 1) * 128],
                WBF[0][:, i * 8:(i + 1) * 8],
                start=(i == 0), stop=(i == DC - 1), skip_group_check=True),
                waits=((ev_wrc, ev_lgprev) if i == 0 else ()),
                inc=PE if i == DC - 1 else None)
        ev_mm = sc.now(PE)
        ev_ev = sc.op("vector", lambda e: e.tensor_copy(SML[:, 0:8], lg_ps[:]),
                      waits=(ev_mm,), inc=DV)
        ev_lgprev = ev_ev
        ev_m8 = sc.op("vector", lambda e: e.max(SML2[:, 0:8], SML[:, 0:8]),
                      waits=(ev_ev,), inc=DV)
        sc.op("vector", lambda e: e.tensor_sub(
            SML2[:, 8:9], SML2[:, 1:2], SML2[:, 0:1]), waits=(ev_m8,))
        ev_d = sc.op("vector", lambda e: e.tensor_sub(
            SML2[:, 9:10], SML2[:, 0:1], SML2[:, 1:2]), inc=DV)
        ev_g2 = sc.op("scalar", lambda e: e.activation(
            out=SML2[:, 10:11], in_=SML2[:, 8:9], func=AF.Sigmoid),
            waits=(ev_d,), inc=AC)
        ev_g1 = sc.op("scalar", lambda e: e.activation(
            out=SML2[:, 11:12], in_=SML2[:, 9:10], func=AF.Sigmoid), inc=AC)
        sc.op("vector", lambda e: e.tensor_scalar(
            out=SML[:, 8:16], in0=SML[:, 0:8], scalar1=SML2[:, 0:1],
            scalar2=None, op0=ALU.is_equal), waits=(ev_g1, ev_g2))
        sc.op("vector", lambda e: e.tensor_scalar(
            out=SML[:, 16:24], in0=SML[:, 0:8], scalar1=SML2[:, 1:2],
            scalar2=None, op0=ALU.is_equal))
        sc.op("vector", lambda e: e.tensor_scalar(
            out=SML[:, 8:16], in0=SML[:, 8:16], scalar1=SML2[:, 11:12],
            scalar2=None, op0=ALU.mult))
        sc.op("vector", lambda e: e.tensor_scalar(
            out=SML[:, 16:24], in0=SML[:, 16:24], scalar1=SML2[:, 10:11],
            scalar2=None, op0=ALU.mult))
        ev_G[tt] = sc.op("vector", lambda e, tt=tt: e.tensor_add(
            GT[:, tt * 8:(tt + 1) * 8], SML[:, 8:16], SML[:, 16:24]), inc=DV)
    dram_tap("G", GT[:], (ev_G[15],))
    sc.guard[("wbf", 0)] = sc.now(PE)
    fence()
    lgh.__exit__(None, None, None)

    eph = [nc.psum_tensor(f"eps{j}", [8, 512]) for j in range(4)]
    eps_ps = [h.__enter__() for h in eph]
    h1, h2 = BUF1, BUF2

    def silu_consume(dst):
        def consume(o, tj, ps_ap, ev_mm, dst=dst):
            return sc.op("scalar", lambda e, o=o, tj=tj: e.activation(
                out=dst[:, o * 1024 + tj * 512:o * 1024 + (tj + 1) * 512],
                in_=ps_ap[:, 0:512], func=AF.Silu), waits=(ev_mm,), inc=AC)
        return consume

    first_g = [True]
    ev_gmm_last = None
    ev_tp_guard = [None]
    prev = (ev_xn0, ev_xnh_st)
    for half in range(2):
        tb = half * 1024

        def rhs_xn(i, tj, tb=tb):
            return xn0[:, i * T + tb + tj * 512:i * T + tb + (tj + 1) * 512]
        ev1 = fm_proj(w1T, INTER, rhs_xn, [512, 512], silu_consume(h1),
                      f"w1{half}", wait_rhs=prev)

        def rhs_h1(i, tj):
            return h1[:, i * 1024 + tj * 512:i * 1024 + (tj + 1) * 512]
        ev3 = fm_proj(w3T, INTER, rhs_h1, [512, 512], silu_consume(h2),
                      f"w3{half}", wait_rhs=(ev1[(DC - 1, 1)],))

        def rhs_h2(i, tj):
            return h2[:, i * 1024 + tj * 512:i * 1024 + (tj + 1) * 512]

        def cons_h(o, tj, ps_ap, ev_mm):
            return sc.op("vector", lambda e, o=o, tj=tj: e.tensor_copy(
                h1[:, o * 1024 + tj * 512:o * 1024 + (tj + 1) * 512],
                ps_ap[:, 0:512]), waits=(ev_mm,), inc=DV)
        evh = fm_proj(w2T, D, rhs_h2, [512, 512], cons_h, f"w2{half}",
                      wait_rhs=(ev3[(DC - 1, 1)],))
        ev_h_all = evh[(DC - 1, 1)]
        tph = nc.psum_tensor(f"tp{half}", [128, 512], BF16)
        tp_ps = tph.__enter__()
        for ttl in range(8):
            tt = half * 8 + ttl
            for d4 in range(4):
                ev_tp = None
                for dd in range(4):
                    d = d4 * 4 + dd
                    ev_tp = sc.op("tensor",
                                  lambda e, d=d, dd=dd, ttl=ttl: e.transpose(
                                      tp_ps[:, dd * 128:(dd + 1) * 128],
                                      h1[:, d * 1024 + ttl * 128:d * 1024 + (ttl + 1) * 128],
                                      IDB[:]),
                                  waits=((ev_h_all, ev_idb, ev_tp_guard[0])
                                         if dd == 0 else ()),
                                  inc=PE if dd == 3 else None)
                ev_ev = sc.op("vector", lambda e, d4=d4: e.tensor_copy(
                    EVB[d4 % 2][:], tp_ps[:]), waits=(ev_tp,), inc=DV)
                ev_tp_guard[0] = ev_ev
                st0 = first_g[0] and True
                if d4 == 0:
                    pass
                last = (tt == 15)
                ev_gmm_last = sc.op("tensor",
                                    lambda e, d4=d4, tt=tt, last=last: e.matmul(
                                        eps_ps[d4][:], GT[:, tt * 8:(tt + 1) * 8],
                                        EVB[d4 % 2][:], start=(tt == 0),
                                        stop=last, skip_group_check=True),
                                    waits=(ev_ev, ev_G[tt]), inc=PE)
        tph.__exit__(None, None, None)
        prev = (evh[(DC - 1, 1)],)
    EPS8 = sbuf_scoped("eps8", [8, 512], F32)
    ev_e_st = None
    for j in range(4):
        ev_eev = sc.op("vector", lambda e, j=j: e.tensor_copy(
            EPS8[0:8, :], eps_ps[j][:]),
            waits=((ev_gmm_last,) if j == 0 else ()) + ((ev_e_st,) if ev_e_st else ()),
            inc=DV)
        ev_e_st = sc.dma("gpsimd", eag_src[0:8, j * 512:(j + 1) * 512],
                         EPS8[0:8, :], waits=(ev_eev,), inc=QSE)
    for h_ in reversed(eph):
        h_.__exit__(None, None, None)
    for s in ALL_G_STORES:
        sc.wait("gpsimd", sc.now(s))
    ev_cc1 = sc.op("gpsimd", lambda e: e.collective_compute(
        "AllReduce", ALU.add, replica_groups=[list(range(8))],
        ins=[eag_src], outs=[eag_dst]), inc=CC1)
    dram_tap("esrc", eag_src, (ev_e_st,))
    sbuf_free("eps8")
    fence()

    if limit <= 1:
        return dev_finish()
    sc.phase = "p23_l0attn"
    MSK = sbuf_scoped("msk", [128, 4096], BF16)
    ev_y_st = {}
    kv_mark = {}
    for chunk in range(2):
        tb = chunk * 1024
        ev_tabs = []
        for (idx, src, dstc, n) in ((0, qa0[:, tb:tb + 1024], 0, 1024),
                                    (1, qb0[:, tb:tb + 1024], 1024, 1024),
                                    (0, ka0[:, chunk * 1536:(chunk + 1) * 1536], 2048, 1536),
                                    (1, kb0[:, chunk * 1536:(chunk + 1) * 1536], 3584, 1536)):
            evl = load_wst(idx, src, ncols=n)
            evc = sc.op("vector", lambda e, idx=idx, dstc=dstc, n=n: e.tensor_copy(
                TAB[:, dstc:dstc + n], WST[idx][:, 0:n]), waits=(evl,), inc=DV)
            wst_used(idx, evc)
            ev_tabs.append(evc)
        ev_tabs = tuple(ev_tabs)
        dram_tap(f"tab{chunk}", TAB[:, 0:5120], ev_tabs)
        ev_xnh_ld = None
        for d in range(DC):
            ev_xnh_ld = sc.dma("sync", BUF1[:, d * 1024:(d + 1) * 1024],
                               xnh_scr[d * 128:(d + 1) * 128, :],
                               waits=(ev_xnh_st,) if d == 0 else (), inc=QH)

        def qdst(o, tj):
            return BUF2[:, o * 1024 + tj * 512:o * 1024 + (tj + 1) * 512]
        evq = fm_proj(wq0T, D,
                      lambda i, tj, tb=tb: xn0[:, i * T + tb + tj * 512:i * T + tb + (tj + 1) * 512],
                      [512, 512],
                      rope_consume(qdst,
                                   lambda o, tj: TAB[:, tj * 512:(tj + 1) * 512],
                                   lambda o, tj: TAB[:, 1024 + tj * 512:1024 + (tj + 1) * 512]),
                      f"q0c{chunk}", wait_rhs=(ev_xn0,) + ev_tabs)

        def rhs_kv(i, tj, tb=tb, chunk=chunk):
            if tj == 0:
                return BUF1[:, i * 1024 + chunk * 512:i * 1024 + chunk * 512 + 512]
            return xn0[:, i * T + tb + (tj - 1) * 512:i * T + tb + tj * 512]

        def kdst(o, tj, chunk=chunk):
            col = chunk * 1536 + tj * 512
            return ("dma", k0_scr[o * 128:(o + 1) * 128, col:col + 512])
        evk = fm_proj(wk0T, D, rhs_kv, [512, 512, 512],
                      rope_consume(kdst,
                                   lambda o, tj: TAB[:, 2048 + tj * 512:2048 + (tj + 1) * 512],
                                   lambda o, tj: TAB[:, 3584 + tj * 512:3584 + (tj + 1) * 512]),
                      f"k0c{chunk}",
                      wait_rhs=(ev_xnh_ld, evq[(DC - 1, 1)]))

        vth = nc.psum_tensor(f"vt{chunk}", [128, 512], BF16)
        vt_ps = vth.__enter__()

        def vout_fn(o, tj, chunk=chunk):
            row0 = chunk * 1536 + tj * 512
            return bass.AP(v0_scr.tensor, row0 * D + o * 128,
                           [[D, 128], [128 * D, 4], [1, 128]])
        fm_proj(wv0T, D, rhs_kv, [512, 512, 512],
                trans_consume(vt_ps, vout_fn, ("vtp", chunk)),
                f"v0c{chunk}", wait_rhs=(evk[(DC - 1, 2)],))
        vth.__exit__(None, None, None)
        kv_mark[chunk] = g_barrier()
        fence()

        sh = [nc.psum_tensor(f"s{chunk}{j}", [128, 1024]) for j in range(2)]
        s_ps = [h_.__enter__() for h_ in sh]
        dnh = nc.psum_tensor(f"dn{chunk}", [1, 512]); dn_ps = dnh.__enter__()
        yh = nc.psum_tensor(f"yp{chunk}", [128, 512]); y_ps = yh.__enter__()
        rbh = nc.psum_tensor(f"rbp{chunk}", [128, 512]); rb_ps = rbh.__enter__()
        ev_exp = {}
        ev_msk = {}
        ev_av = {}
        ev_recip_prev = None
        ev_rbev_prev = None
        ev_y_prev = None
        ev_mld = {}
        for qg in range(2):
            mwait = ev_msk.get((qg - 1, DC - 1, 3))
            ev_mld[qg] = sc.dma(
                "sync", MSK[:].rearrange("p (j n) -> p j n", j=4),
                bass.AP(maskl0.tensor, (chunk * 8 + qg * 4) * 128 * 1024,
                        [[1024, 128], [128 * 1024, 4], [1, 1024]]),
                waits=(mwait,), inc=QM)
            for h in range(DC):
                hq = qg * DC + h
                bsl = hq % 2
                col0 = chunk * 1536 + qg * 512
                ev_kld = sc.dma(
                    "sync", KWB[bsl][:],
                    k0_scr[h * 128:(h + 1) * 128, col0:col0 + 1024],
                    waits=(kv_mark[chunk], ev_exp.get((hq - 2, 3))), inc=QK[bsl])
                ev_vld = sc.dma(
                    "sync", VWB[bsl][:].rearrange("p (k d) -> p k d", k=8),
                    bass.AP(v0_scr.tensor, col0 * D + h * 128,
                            [[D, 128], [128 * D, 8], [1, 128]]),
                    waits=(kv_mark[chunk], ev_av.get((hq - 2, 3))), inc=QV[bsl])
                def emit_scores(p, bsl=bsl, h=h, qg=qg, hq=hq, ev_kld=ev_kld):
                    for half in range(2):
                        kt = 2 * p + half
                        if half == 0:
                            if p < 2:
                                w = [ev_kld, ev_exp.get((hq - 1, 2 + p))]
                            else:
                                w = [ev_exp.get((hq, p - 2))]
                        else:
                            w = []
                        sc.op("tensor", lambda e, bsl=bsl, p=p, half=half, kt=kt, h=h, qg=qg: e.matmul(
                            s_ps[p % 2][:, half * 512:(half + 1) * 512],
                            KWB[bsl][:, kt * 128:(kt + 1) * 128],
                            BUF2[:, h * 1024 + qg * 512:h * 1024 + qg * 512 + 512],
                            start=True, stop=True, skip_group_check=True),
                            waits=w, inc=PE if half == 1 else None)
                    return sc.now(PE)

                ev_sc = {0: emit_scores(0), 1: emit_scores(1)}
                for p in range(4):
                    aw = (ev_av.get((hq, p - 2)) if p >= 2
                          else ev_av.get((hq - 1, 2 + p)))
                    ev_exp[(hq, p)] = sc.op(
                        "scalar", lambda e, p=p: e.activation(
                            out=PMB[p % 2][:], in_=s_ps[p % 2][:], func=AF.Exp),
                        waits=(ev_sc[p], aw), inc=AC)
                    ev_msk[(qg, h, p)] = sc.op(
                        "vector", lambda e, p=p: e.tensor_tensor(
                            out=PMB[p % 2][:], in0=PMB[p % 2][:],
                            in1=MSK[:, p * 1024:(p + 1) * 1024], op=ALU.mult),
                        waits=(ev_exp[(hq, p)], ev_mld[qg]), inc=DV)
                    for half in range(2):
                        kt = 2 * p + half
                        sc.op("tensor", lambda e, p=p, half=half: e.matmul(
                            dn_ps[:], ONESC[:],
                            PMB[p % 2][:, half * 512:(half + 1) * 512],
                            start=(p == 0 and half == 0),
                            stop=(p == 3 and half == 1), skip_group_check=True),
                            waits=((ev_msk[(qg, h, p)], ev_recip_prev)
                                   if half == 0 else ()))
                        ev_av[(hq, p)] = sc.op(
                            "tensor", lambda e, p=p, half=half, kt=kt, bsl=bsl: e.matmul(
                                y_ps[:], VWB[bsl][:, kt * 128:(kt + 1) * 128],
                                PMB[p % 2][:, half * 512:(half + 1) * 512],
                                start=(p == 0 and half == 0),
                                stop=(p == 3 and half == 1),
                                skip_group_check=True),
                            waits=((ev_vld, ev_y_prev)
                                   if (p == 0 and half == 0) else ()), inc=PE)
                    if p + 2 <= 3:
                        ev_sc[p + 2] = emit_scores(p + 2)
                ev_rc = sc.op("vector", lambda e: e.reciprocal(
                    RS[0:1, 512:1024], dn_ps[:]),
                    waits=(ev_av[(hq, 3)], ev_rbev_prev), inc=DV)
                ev_recip_prev = ev_rc
                ev_rbmm = sc.op("tensor", lambda e: e.matmul(
                    rb_ps[:], ONES1F[0:1, :], RS[0:1, 512:1024],
                    start=True, stop=True, skip_group_check=True),
                    waits=(ev_rc,), inc=PE)
                ev_rbev = sc.op("vector", lambda e, bsl=bsl: e.tensor_copy(
                    RBQ[bsl][:], rb_ps[:]), waits=(ev_rbmm,), inc=DV)
                ev_rbev_prev = ev_rbev
                ev_y = sc.op("vector", lambda e, bsl=bsl: e.tensor_tensor(
                    out=RP[2][:], in0=y_ps[:], in1=RBQ[bsl][:], op=ALU.mult),
                    waits=(ev_rbev, ev_y_st.get((chunk, hq - 1))), inc=DV)
                ev_y_prev = ev_y
                ev_y_st[(chunk, hq)] = sc.dma(
                    "gpsimd",
                    y0_scr[h * 128:(h + 1) * 128, tb + qg * 512:tb + qg * 512 + 512],
                    RP[2][:], waits=(ev_y,), inc=QSY)
        sc.rope_ev = ev_y_st[(chunk, 2 * DC - 1)]
        fence()
        for h_ in [rbh, yh, dnh] + sh[::-1]:
            h_.__exit__(None, None, None)
    y_mark = g_barrier()
    sbuf_free("msk")
    dram_tap("k0", k0_scr, (y_mark,))
    dram_tap("v0", v0_scr, (y_mark,))
    dram_tap("y0", y0_scr, (y_mark,))

    if limit <= 3:
        return dev_finish()
    sc.phase = "p4_l0epi"
    E8 = sbuf_scoped("e8", [8, 512], F32)
    XT = [sbuf_scoped(f"xt{i}", [128, 512], F32) for i in range(2)]
    ev_yld = None
    for d in range(DC):
        ev_yld = sc.dma("sync", BIGA[:, d * T:(d + 1) * T],
                        y0_scr[d * 128:(d + 1) * 128, :],
                        waits=(y_mark,) if d == 0 else (), inc=QH)
    eth = nc.psum_tensor("etps", [128, 8]); et_ps = eth.__enter__()
    ev_esum = {}
    for qtr in range(4):
        ev_e8 = sc.dma("sync", E8[0:8, 0:512],
                       eag_dst[0:8, qtr * 512:(qtr + 1) * 512],
                       waits=((CC1, 1), ev_esum.get(4 * qtr - 1)), inc=QE)
        for oo in range(4):
            o = qtr * 4 + oo
            ev_t = sc.op("tensor", lambda e, oo=oo: e.transpose(
                et_ps[:], E8[0:8, oo * 128:(oo + 1) * 128], IDF8[:]),
                waits=(ev_e8, ev_esum.get(o - 1)), inc=PE)
            ev_esum[o] = sc.op("vector", lambda e, o=o: e.tensor_scalar(
                out=ESUM[:, o * 8:(o + 1) * 8], in0=et_ps[:], scalar1=EGT[:, 0:1],
                scalar2=None, op0=ALU.mult), waits=(ev_t,), inc=DV)

    ev_x1_st = {}

    def epi0_consume(o, tj, ps_ap, ev_mm):
        b = tj % 2
        prev = ev_x1_st.get((o, tj - 2)) or ev_x1_st.get((o - 1, tj + 2))
        evx = sc.dma("sync", XT[b][:],
                     x0_fm[o * 128:(o + 1) * 128, tj * 512:(tj + 1) * 512],
                     waits=(prev,), inc=QX[b])
        ev = sc.op("vector", lambda e, b=b: e.tensor_tensor(
            out=XT[b][:], in0=ps_ap[:, 0:512], in1=XT[b][:],
            op=ALU.add), waits=(ev_mm, evx), inc=DV)
        if tj == 0:
            ev = sc.op("vector", lambda e, o=o, b=b: e.tensor_tensor(
                out=XT[b][:, 0:8], in0=XT[b][:, 0:8],
                in1=ESUM[:, o * 8:(o + 1) * 8], op=ALU.add),
                waits=(ev_esum[o],), inc=DV)
        ev_st = sc.dma("gpsimd",
                       x1_scr[o * 128:(o + 1) * 128, tj * 512:(tj + 1) * 512],
                       XT[b][:], waits=(ev,), inc=QSX[b])
        ev_x1_st[(o, tj)] = ev_st
        return ev

    fm_proj(wo0T, D,
            lambda i, tj: BIGA[:, i * T + tj * 512:i * T + (tj + 1) * 512],
            [512, 512, 512, 512], epi0_consume, "epi0", wait_rhs=(ev_yld,))
    x1_mark = g_barrier()
    dram_tap("x1", x1_scr, (x1_mark,))
    fence()
    eth.__exit__(None, None, None)
    sbuf_free("xt1", "xt0", "e8")

    if limit <= 4:
        return dev_finish()
    sc.phase = "p5_ln1"
    sc.wait("sync", x1_mark)
    xn1 = BIGA
    ev_xn1 = emit_ln(lambda d: x1_scr[d * 128:(d + 1) * 128, :], xn1, T, "l1")
    fence()
    dram_tap("xn1", xn1[:], (ev_xn1,))

    if limit <= 5:
        return dev_finish()
    sc.phase = "p6_l1pre"
    QABS = sbuf_scoped("qabs_sb", [128, 2048], F32)
    KABS = sbuf_scoped("kabs_sb", [128, 64], F32)
    sc.op("gpsimd", lambda e: e.iota(KABS[:], [[128, 64]], channel_multiplier=1,
                                     allow_small_or_imprecise_dtypes=True))
    ev_bc = None
    for hkv in range(4):
        evl = sc.dma("sync", RS[0:1, 0:512], bq1[0:1, hkv * 512:(hkv + 1) * 512],
                     waits=(ev_bc,), inc=QR)
        sc.wait("vector", evl)
        sc.op("vector", lambda e: e.tensor_add(
            RS[0:1, 512:640], RS[0:1, 0:128], RS[0:1, 128:256]))
        sc.op("vector", lambda e: e.tensor_add(
            RS[0:1, 512:640], RS[0:1, 512:640], RS[0:1, 256:384]))
        ev_s = sc.op("vector", lambda e: e.tensor_add(
            RS[0:1, 512:640], RS[0:1, 512:640], RS[0:1, 384:512]), inc=DV)
        ev_bc = sc.op("scalar", lambda e, hkv=hkv: e.activation(
            out=BIASR[0:1, hkv * 128:(hkv + 1) * 128], in_=RS[0:1, 512:640],
            func=AF.Copy), waits=(ev_s,), inc=AC)
    ev_l = sc.dma("sync", RS[0:1, 0:512], bk1, waits=(ev_bc,), inc=QR)
    ev_bc = sc.op("scalar", lambda e: e.activation(
        out=BIASR[0:1, 512:1024], in_=RS[0:1, 0:512], func=AF.Copy),
        waits=(ev_l,), inc=AC)
    ev_l = sc.dma("sync", RS[0:1, 0:512], bv1, waits=(ev_bc,), inc=QR)
    ev_bc = sc.op("scalar", lambda e: e.activation(
        out=BIASR[0:1, 1024:1536], in_=RS[0:1, 0:512], func=AF.Copy),
        waits=(ev_l,), inc=AC)
    ev_b32 = None
    ev_b64 = None
    for c4 in range(4):
        evl = sc.dma("sync", RS[32:33, 0:512], bo1[0:1, c4 * 512:(c4 + 1) * 512],
                     waits=(ev_b32, ev_bc), inc=QR)
        ev_b32 = sc.op("scalar", lambda e, c4=c4: e.activation(
            out=BIASR[32:33, c4 * 512:(c4 + 1) * 512], in_=RS[32:33, 0:512],
            func=AF.Copy), waits=(evl,), inc=AC)
        evl = sc.dma("sync", RS[64:65, 0:512], bb1[0:1, c4 * 512:(c4 + 1) * 512],
                     waits=(ev_b64, ev_b32), inc=QR)
        ev_b64 = sc.op("scalar", lambda e, c4=c4: e.activation(
            out=BIASR[64:65, c4 * 512:(c4 + 1) * 512], in_=RS[64:65, 0:512],
            func=AF.Copy), waits=(evl,), inc=AC)
    ev_bias = sc.op("scalar", lambda e: e.activation(
        out=FSC[0:1, 4:5], in_=FSC[0:1, 0:1], func=AF.Copy),
        waits=(ev_bc, ev_b32, ev_b64), inc=AC)

    ev_wq_st = None
    for i in range(DC):
        evl = load_wst(i % 2, wq1[i * 128:(i + 1) * 128, :])

        def wv4(off, i=i):
            return bass.AP(WST[i % 2], off, [[2048, 128], [512, 4], [1, 128]])
        sc.wait("vector", evl)
        ev_add = None
        for j in (1, 2, 3):
            ev_add = sc.op("vector", lambda e, wv4=wv4, j=j: e.tensor_add(
                wv4(0), wv4(0), wv4(j * 128)), inc=DV if j == 3 else None)
        ev_cast = sc.op("scalar", lambda e, wv4=wv4: e.activation(
            out=RP[0][:], in_=wv4(0), func=AF.Copy),
            waits=(ev_add, sc.rope_ev), inc=AC)
        wst_used(i % 2, ev_cast)
        ev_wq_st = sc.dma("gpsimd", wqp_scr[i * 128:(i + 1) * 128, :], RP[0][:],
                          waits=(ev_cast, ev_wq_st), inc=QSR)
        sc.rope_ev = ev_wq_st
    wq_mark = g_barrier()
    dram_tap("wqp", wqp_scr, (wq_mark,))

    ev_tl = []
    for idx, tsrc in enumerate((ka1, kb1)):
        evl = load_wst(idx % 2, tsrc)
        evc = sc.op("vector", lambda e, idx=idx: e.tensor_copy(
            TAB[:, idx * 2048:(idx + 1) * 2048], WST[idx % 2][:]),
            waits=(evl,), inc=DV)
        wst_used(idx % 2, evc)
        ev_tl.append(evc)
    ev_tabs1 = tuple(ev_tl)

    def k1dst(o, tj):
        return ("dma", kvag_src[o * 128:(o + 1) * 128, tj * 512:(tj + 1) * 512])
    evk1 = fm_proj(wk1, 512,
                   lambda i, tj: xn1[:, i * T + tj * 512:i * T + (tj + 1) * 512],
                   [512, 512, 512, 512],
                   rope_consume(k1dst,
                                lambda o, tj: TAB[:, tj * 512:(tj + 1) * 512],
                                lambda o, tj: TAB[:, 2048 + tj * 512:2048 + (tj + 1) * 512]),
                   "k1", wait_rhs=(ev_xn1, ev_bias) + ev_tabs1,
                   bias_slice=lambda o: BIASR[0:1, 512 + o * 128:512 + (o + 1) * 128])

    vt1h = nc.psum_tensor("vt1p", [128, 512], BF16)
    vt1_ps = vt1h.__enter__()

    def v1out_fn(o, tj):
        off = 512 * D + (tj * 512) * 512 + o * 128
        return bass.AP(kvag_src.tensor, off, [[512, 128], [128 * 512, 4], [1, 128]])
    evv1 = fm_proj(wv1, 512,
                   lambda i, tj: xn1[:, i * T + tj * 512:i * T + (tj + 1) * 512],
                   [512, 512, 512, 512],
                   trans_consume(vt1_ps, v1out_fn, ("vtp", 9)),
                   "v1", wait_rhs=(evk1[(3, 3)],),
                   bias_slice=lambda o: BIASR[0:1, 1024 + o * 128:1024 + (o + 1) * 128])
    vt1h.__exit__(None, None, None)
    for s in ALL_G_STORES:
        sc.wait("gpsimd", sc.now(s))
    ev_cc2 = sc.op("gpsimd", lambda e: e.collective_compute(
        "AllGather", ALU.bypass,
        replica_groups=[[0, 1, 2, 3], [4, 5, 6, 7]],
        ins=[kvag_src], outs=[kvag_dst]), inc=CC2)
    dram_tap("kvsrc", kvag_src, (sc.now(CC2),))

    sc.wait("sync", wq_mark)
    ev_tlq = []
    for idx, tsrc in enumerate((qa1, qb1)):
        evl = load_wst(idx % 2, tsrc)
        evc = sc.op("vector", lambda e, idx=idx: e.tensor_copy(
            TAB[:, idx * 2048:(idx + 1) * 2048], WST[idx % 2][:]),
            waits=(evl, evk1[(3, 3)]), inc=DV)
        wst_used(idx % 2, evc)
        ev_tlq.append(evc)

    def q1dst(o, tj):
        return BUF1[:, o * 2048 + tj * 512:o * 2048 + (tj + 1) * 512]
    evq1 = fm_proj(wqp_scr, 512,
                   lambda i, tj: xn1[:, i * T + tj * 512:i * T + (tj + 1) * 512],
                   [512, 512, 512, 512],
                   rope_consume(q1dst,
                                lambda o, tj: TAB[:, tj * 512:(tj + 1) * 512],
                                lambda o, tj: TAB[:, 2048 + tj * 512:2048 + (tj + 1) * 512]),
                   "q1", wait_rhs=(evv1[(3, 3)],) + tuple(ev_tlq),
                   bias_slice=lambda o: BIASR[0:1, o * 128:(o + 1) * 128],
                   src_bf16=True)
    ev_q1 = evq1[(3, 3)]
    dram_tap("q1", BUF1[:, 0:8192], (ev_q1,))

    qbh = nc.psum_tensor("qbps", [128, 512]); qb_ps = qbh.__enter__()
    ev_qb_prev = None
    ev_qld_prev = None
    for s in range(4):
        ev_ql = sc.dma("sync", RS[0:1, 0:512], qabs_i[s:s + 1, :],
                       waits=(ev_qld_prev,), inc=QR)
        ev_q = sc.op("tensor", lambda e, s=s: e.matmul(
            qb_ps[:], ONES1F[0:1, :], RS[0:1, 0:512],
            start=True, stop=True, skip_group_check=True),
            waits=(ev_ql, ev_qb_prev), inc=PE)
        ev_qld_prev = ev_q
        ev_qb_prev = sc.op("vector", lambda e, s=s: e.tensor_copy(
            QABS[:, s * 512:(s + 1) * 512], qb_ps[:]), waits=(ev_q,), inc=DV)
    ev_qabs = ev_qb_prev
    qbh.__exit__(None, None, None)
    dram_tap("qabs_t", QABS[:], (ev_qabs,))

    lg2h = nc.psum_tensor("lg2", [128, 8]); lg2_ps = lg2h.__enter__()
    ohh = nc.psum_tensor("ohps", [128, 128], BF16); oh_ps = ohh.__enter__()
    ev_wr2 = load_wst(0, bass.AP(rwT.tensor, 0, [[8, 128], [128 * 8, DC], [1, 8]]),
                      ncols=8 * DC)
    ev_wrc2 = sc.op("vector", lambda e: e.tensor_copy(
        WBF[0][:, 0:8 * DC], WST[0][:, 0:8 * DC]),
        waits=(ev_wr2, sc.guard.get(("wbf", 0))), inc=DV)
    wst_used(0, ev_wrc2)
    ev_mfm = {}
    ev_ohprev = None
    for tt in range(16):
        for i in range(DC):
            sc.op("tensor", lambda e, tt=tt, i=i: e.matmul(
                lg2_ps[:], xn1[:, i * T + tt * 128:i * T + (tt + 1) * 128],
                WBF[0][:, i * 8:(i + 1) * 8],
                start=(i == 0), stop=(i == DC - 1), skip_group_check=True),
                waits=((ev_wrc2, ev_xn1, ev_mfm.get(tt - 1)) if i == 0 else ()),
                inc=PE if i == DC - 1 else None)
        ev_mm = sc.now(PE)
        ev_ev = sc.op("vector", lambda e: e.tensor_copy(SML[:, 0:8], lg2_ps[:]),
                      waits=(ev_mm,), inc=DV)
        ev_m8 = sc.op("vector", lambda e: e.max(SML2[:, 0:8], SML[:, 0:8]),
                      waits=(ev_ev,), inc=DV)
        ev_oh = sc.op("vector", lambda e: e.tensor_scalar(
            out=OHB[:, 0:8], in0=SML[:, 0:8], scalar1=SML2[:, 0:1],
            scalar2=None, op0=ALU.is_equal), waits=(ev_m8, ev_ohprev), inc=DV)
        ev_ex = sc.op("vector", lambda e: e.tensor_copy(
            OHB[:, 0:128].rearrange("p (e r) -> p e r", e=8),
            bass.AP(OHB, 0, [[128, 128], [1, 8], [0, 16]])),
            waits=(ev_oh,), inc=DV)
        ev_tp = sc.op("tensor", lambda e: e.transpose(
            oh_ps[:], OHB[:, 0:128], IDB[:]),
            waits=(ev_ex, ev_mfm.get(tt - 1)), inc=PE)
        ev_mfm[tt] = sc.op("vector", lambda e, tt=tt: e.tensor_copy(
            BUF2[:, 2048 + tt * 128:2048 + (tt + 1) * 128], oh_ps[:]),
            waits=(ev_tp,), inc=DV)
        ev_ohprev = ev_mfm[tt]
    ev_maskfm = ev_mfm[15]
    sc.guard[("wbf", 0)] = sc.now(PE)
    ohh.__exit__(None, None, None)
    lg2h.__exit__(None, None, None)
    dram_tap("maskfm", BUF2[:, 2048:4096], (ev_maskfm,))

    ev_at_l = load_wst(0, colblock_ap(aT, 128, 0), ncols=DC * 128)
    ev_at_c = sc.op("vector", lambda e: e.tensor_copy(
        BUF2[:, 14336:16384], WST[0][:, 0:DC * 128]), waits=(ev_at_l,), inc=DV)
    wst_used(0, ev_at_c)
    hah = nc.psum_tensor("haps", [128, 512]); ha_ps = hah.__enter__()
    ev_hm = {}
    for tj in range(4):
        for i in range(DC):
            sc.op("tensor", lambda e, tj=tj, i=i: e.matmul(
                ha_ps[:], BUF2[:, 14336 + i * 128:14336 + (i + 1) * 128],
                xn1[:, i * T + tj * 512:i * T + (tj + 1) * 512],
                start=(i == 0), stop=(i == DC - 1), skip_group_check=True),
                waits=((ev_at_c, ev_hm.get(tj - 1)) if i == 0 else ()),
                inc=PE if i == DC - 1 else None)
        ev_mm = sc.now(PE)
        ev_hev = sc.op("vector", lambda e, tj=tj: e.tensor_copy(
            EVB[0][:], ha_ps[:]), waits=(ev_mm,), inc=DV)
        ev_hm[tj] = sc.op("vector", lambda e, tj=tj: e.tensor_tensor(
            out=BUF2[:, tj * 512:(tj + 1) * 512], in0=EVB[0][:],
            in1=BUF2[:, 2048 + tj * 512:2048 + (tj + 1) * 512], op=ALU.mult),
            waits=(ev_hev, ev_maskfm), inc=DV)
    ev_hmask = ev_hm[3]
    hah.__exit__(None, None, None)
    dram_tap("hmask", BUF2[:, 0:2048], (ev_hmask,))

    ev_wo1 = None
    for cch in range(4):
        evl = load_wst(cch % 2, wo1[cch * 128:(cch + 1) * 128, :])
        ev_wo1 = sc.op("vector", lambda e, cch=cch: e.tensor_copy(
            BUF2[:, 4096 + cch * 2048:4096 + (cch + 1) * 2048],
            WST[cch % 2][:]), waits=(evl,), inc=DV)
        wst_used(cch % 2, ev_wo1)
    ev_bw_l = load_wst(0, bwT_i)
    ev_bw = sc.op("scalar", lambda e: e.activation(
        out=BUF2[:, 12288:14336], in_=WST[0][:], func=AF.Copy, scale=2.0),
        waits=(ev_bw_l,), inc=AC)
    wst_used(0, ev_bw)
    fence()
    psum_free()

    if limit <= 6:
        return dev_finish()
    sc.phase = "p7_l1attn"
    sh = [nc.psum_tensor(f"sl1_{j}", [128, 1024]) for j in range(2)]
    s_ps = [h_.__enter__() for h_ in sh]
    dnh = nc.psum_tensor("dnl1", [1, 512]); dn_ps = dnh.__enter__()
    yh = nc.psum_tensor("ypl1", [128, 512]); y_ps = yh.__enter__()
    rbh = nc.psum_tensor("rbl1", [128, 512]); rb_ps = rbh.__enter__()
    KT = [e // 128 for e in ES]

    def kmap(kt):
        kabs = kt * 128
        j = kabs // 1024
        return min(j, 7 - j), kabs % 1024 + (1024 if j > 3 else 0)

    ev_exp = {}
    ev_stt = {}
    ev_av = {}
    ev_recip_prev = None
    ev_rbev_prev = None
    ev_y1_done = {}
    sc.wait("sync", (CC2, 1))
    dram_tap("kvag", kvag_dst, ((CC2, 1),))
    import os as _os
    _only = _os.environ.get("K_L1_ONLY")
    hs = -1
    for h in range(4):
        for s in range(4):
            hs += 1
            if _only is not None and (h, s) != tuple(map(int, _only.split(","))):
                continue
            nkt = KT[s]
            if _os.environ.get("K_L1_MAXKT"):
                nkt = min(nkt, int(_os.environ["K_L1_MAXKT"]))
            npair = nkt // 2
            nblk = nkt // 8 + (1 if nkt % 8 else 0)
            ev_kld = {}
            ev_vld = {}

            def emit_blk_load(blk, hs=hs, h=h):
                rank, col = kmap(blk * 8)
                bsl = blk % 2
                kw = (ev_exp[(hs, (blk - 2) * 4 + 3)] if blk >= 2
                      else ev_exp.get((hs - 1, "last")))
                ev_kld[blk] = sc.dma(
                    "sync", KWB[bsl][:],
                    kvag_dst[rank * 1024 + h * 128:rank * 1024 + (h + 1) * 128,
                             col:col + 1024],
                    waits=(kw,), inc=QK[bsl])
                vw = (ev_av[(hs, (blk - 2) * 8 + 7)] if blk >= 2
                      else ev_av.get((hs - 1, "last")))
                voff = (rank * 1024 + 512) * D + col * 512 + h * 128
                ev_vld[blk] = sc.dma(
                    "sync", VWB[bsl][:].rearrange("p (k d) -> p k d", k=8),
                    bass.AP(kvag_dst.tensor, voff,
                            [[512, 128], [128 * 512, 8], [1, 128]]),
                    waits=(vw,), inc=QV[bsl])

            for blk in range(min(nblk, 2)):
                emit_blk_load(blk)

            def emit_scores(p, hs=hs, h=h, s=s):
                blk = (2 * p) // 8
                bsl = blk % 2
                for half in range(2):
                    kt = 2 * p + half
                    ktl = kt % 8
                    if half == 0:
                        w = [ev_kld[blk]]
                        if p >= 2:
                            w.append(ev_exp.get((hs, p - 2)))
                        else:
                            w.append(ev_stt.get((hs - 1, "last")))
                    else:
                        w = []
                    sc.op("tensor", lambda e, bsl=bsl, half=half, ktl=ktl, h=h, s=s, p=p: e.matmul(
                        s_ps[p % 2][:, half * 512:(half + 1) * 512],
                        KWB[bsl][:, ktl * 128:(ktl + 1) * 128],
                        BUF1[:, h * 2048 + s * 512:h * 2048 + s * 512 + 512],
                        start=True, stop=True, skip_group_check=True),
                        waits=w, inc=PE if half == 1 else None)
                return sc.now(PE)

            ev_sc = {0: emit_scores(0)}
            if npair > 1:
                ev_sc[1] = emit_scores(1)
            for p in range(npair):
                blk = (2 * p) // 8
                bsl = blk % 2
                aw = (ev_av.get((hs, 2 * (p - 2) + 1)) if p >= 2
                      else ev_av.get((hs - 1, "last")))
                ev_exp[(hs, p)] = sc.op(
                    "scalar", lambda e, p=p: e.activation(
                        out=PMB[p % 2][:], in_=s_ps[p % 2][:], func=AF.Exp),
                    waits=(ev_sc[p], aw), inc=AC)
                for half in range(2):
                    kt = 2 * p + half
                    ev_stt[(hs, p, half)] = sc.op(
                        "vector", lambda e, p=p, half=half, kt=kt, s=s: e.scalar_tensor_tensor(
                            out=PMB[p % 2][:, half * 512:(half + 1) * 512],
                            in0=QABS[:, s * 512:(s + 1) * 512],
                            scalar=KABS[:, kt:kt + 1],
                            in1=PMB[p % 2][:, half * 512:(half + 1) * 512],
                            op0=ALU.is_ge, op1=ALU.mult),
                        waits=(ev_exp[(hs, p)], ev_qabs), inc=DV)
                ev_stt[(hs, p)] = ev_stt[(hs, p, 1)]
                for half in range(2):
                    kt = 2 * p + half
                    ktl = kt % 8
                    sc.op("tensor", lambda e, p=p, half=half: e.matmul(
                        dn_ps[:], ONESC[:],
                        PMB[p % 2][:, half * 512:(half + 1) * 512],
                        start=(p == 0 and half == 0),
                        stop=(p == npair - 1 and half == 1),
                        skip_group_check=True),
                        waits=(ev_stt[(hs, p, half)],) +
                              ((ev_recip_prev,) if (p == 0 and half == 0) else ()))
                    avw = []
                    if ktl == 0 and half == 0:
                        avw.append(ev_vld[blk])
                    if p == 0 and half == 0:
                        avw.append(ev_y1_done.get(hs - 1))
                    ev_av[(hs, kt)] = sc.op(
                        "tensor", lambda e, p=p, half=half, ktl=ktl, bsl=bsl: e.matmul(
                            y_ps[:], VWB[bsl][:, ktl * 128:(ktl + 1) * 128],
                            PMB[p % 2][:, half * 512:(half + 1) * 512],
                            start=(p == 0 and half == 0),
                            stop=(p == npair - 1 and half == 1),
                            skip_group_check=True),
                        waits=avw, inc=PE)
                if p % 4 == 3 and (p - 3) // 4 + 2 < nblk:
                    emit_blk_load((p - 3) // 4 + 2)
                if p + 2 < npair:
                    ev_sc[p + 2] = emit_scores(p + 2)
            ev_exp[(hs, "last")] = ev_exp[(hs, npair - 1)]
            ev_stt[(hs, "last")] = ev_stt[(hs, npair - 1)]
            ev_av[(hs, "last")] = ev_av[(hs, nkt - 1)]
            ev_rc = sc.op("vector", lambda e: e.reciprocal(
                RS[0:1, 512:1024], dn_ps[:]),
                waits=(ev_av[(hs, "last")], ev_rbev_prev), inc=DV)
            ev_recip_prev = ev_rc
            ev_rbmm = sc.op("tensor", lambda e: e.matmul(
                rb_ps[:], ONES1F[0:1, :], RS[0:1, 512:1024],
                start=True, stop=True, skip_group_check=True),
                waits=(ev_rc,), inc=PE)
            ev_rbev = sc.op("vector", lambda e, hs=hs: e.tensor_copy(
                RBQ[hs % 2][:], rb_ps[:]), waits=(ev_rbmm,), inc=DV)
            ev_rbev_prev = ev_rbev
            ev_y1_done[hs] = sc.op(
                "vector", lambda e, h=h, s=s, hs=hs: e.tensor_tensor(
                    out=BUF1[:, 8192 + h * 2048 + s * 512:8192 + h * 2048 + (s + 1) * 512],
                    in0=y_ps[:], in1=RBQ[hs % 2][:], op=ALU.mult),
                waits=(ev_rbev,), inc=DV)
    ev_y1 = ev_y1_done[max(ev_y1_done)]
    dram_tap("y1", BUF1[:, 8192:16384], (ev_y1,))
    fence()
    for h_ in [rbh, yh, dnh] + sh[::-1]:
        h_.__exit__(None, None, None)
    sbuf_free("kabs_sb", "qabs_sb")

    if limit <= 7:
        return dev_finish()
    sc.phase = "p8_l1epi"
    dram_tap("xn1f", xn1[:], ())
    dram_tap("buf2f", BUF2[:], ())
    dram_tap("x1f", x1_scr[0:D, 0:T], ())
    XTE = [sbuf_scoped(f"xte{i}", [128, 512], F32) for i in range(2)]
    BCOL = sbuf_scoped("bcol", [128, 32], F32)
    ev_bo = sc.dma("sync", BCOL[:, 0:16], bocol, inc=QMS)
    ev_bb = sc.dma("sync", BCOL[:, 16:32], bbcol, inc=QMS)
    ev_bcol = sc.op("vector", lambda e: e.tensor_add(
        BCOL[:, 0:16], BCOL[:, 0:16], BCOL[:, 16:32]),
        waits=(ev_bo, ev_bb), inc=DV)
    ev_out_st = {}

    def epi1_extra(o, tj, ps_ap, tw):
        sc.op("tensor", lambda e, o=o, tj=tj: e.matmul(
            ps_ap[:, 0:512], BUF2[:, 12288 + o * 128:12288 + (o + 1) * 128],
            BUF2[:, tj * 512:(tj + 1) * 512],
            start=False, stop=False, skip_group_check=True),
            waits=(ev_hmask, ev_bw) if (o == 0 and tj == 0) else ())
        for cch in range(4):
            sc.op("tensor", lambda e, o=o, tj=tj, cch=cch: e.matmul(
                ps_ap[:, 0:512],
                BUF2[:, 4096 + cch * 2048 + o * 128:4096 + cch * 2048 + (o + 1) * 128],
                BUF1[:, 8192 + cch * 2048 + tj * 512:8192 + cch * 2048 + (tj + 1) * 512],
                start=False, stop=(cch == 3), skip_group_check=True),
                waits=(ev_wo1, ev_y1) if (o == 0 and tj == 0 and cch == 0) else (),
                inc=PE if cch == 3 else None)

    def epi1_consume(o, tj, ps_ap, ev_mm):
        b = tj % 2
        prev = ev_out_st.get((o, tj - 2)) or ev_out_st.get((o - 1, tj + 2))
        evx = sc.dma("sync", XTE[b][:],
                     x1_scr[o * 128:(o + 1) * 128, tj * 512:(tj + 1) * 512],
                     waits=(prev,), inc=QX[b])
        ev = sc.op("vector", lambda e, b=b, o=o, ps_ap=ps_ap: e.scalar_tensor_tensor(
            out=XTE[b][:], in0=ps_ap[:, 0:512], scalar=BCOL[:, o:o + 1],
            in1=XTE[b][:], op0=ALU.add, op1=ALU.add),
            waits=(ev_mm, evx, ev_bcol), inc=DV)
        ev_out_st[(o, tj)] = sc.dma(
            "gpsimd", xout[o * 128:(o + 1) * 128, tj * 512:(tj + 1) * 512],
            XTE[b][:], waits=(ev,), inc=QSO[b])
        return ev

    fm_proj(wbT, D,
            lambda i, tj: xn1[:, i * T + tj * 512:i * T + (tj + 1) * 512],
            [512, 512, 512, 512], epi1_consume, "epi1",
            wait_rhs=(ev_bias,),
            extra_fn=None if _os.environ.get("K_EPI_NOEXTRA") else epi1_extra)
    for s in ALL_G_STORES:
        sc.wait("gpsimd", sc.now(s))

    with nc.Block() as block:
        sc.finalize(block)
    sc.close()
    psum_free()
    for h_ in reversed(list(sb_scoped.values())):
        h_.__exit__(None, None, None)
    sb_scoped.clear()
    for h_ in reversed(ctx):
        h_.__exit__(None, None, None)
    return nc


def kernel(**inputs):
    nc = build_program(DEBUG_TAPS)
    in_maps = [_host_core_inputs(c, inputs) for c in range(NCORE)]
    res = run_bass_kernel_spmd(nc, in_maps, core_ids=list(range(NCORE)))
    out = np.empty((2, 8192, 2048), np.float32)
    for core in range(NCORE):
        b, c = core // 4, core % 4
        pA, pB = c * CH, (7 - c) * CH
        shard = np.asarray(res.results[core]["xout"])
        out[b, pA:pA + CH, :] = shard[:, 0:CH].T
        out[b, pB:pB + CH, :] = shard[:, CH:2 * CH].T
    if DEBUG_TAPS:
        kernel.taps = [{k: np.asarray(v) for k, v in r.items()
                        if k.startswith("tap_")} for r in res.results]
    return out
